# version 11
# baseline (speedup 1.0000x reference)
import numpy as np
import ml_dtypes
import concourse.bass as bass
import concourse.mybir as mybir
from concourse.bass_utils import run_bass_kernel_spmd
from contextlib import ExitStack

F32 = mybir.dt.float32
F16 = mybir.dt.float16
BF16 = mybir.dt.bfloat16
I32 = mybir.dt.int32
AF = mybir.ActivationFunctionType
OP = mybir.AluOpType
AX = mybir.AxisListType

P = 128
D = 2048
NDT = D // P
NH = 16
IO = 5632
NIT = IO // P
NPAIR = IO // P
SCH = 64
TC = 512
NTT = TC // P
GUW = D // 8
DNW = IO // 8
DQW = 256
TCA = 1024
NTTA = TCA // P
NWIN = 4

ENGS = ("sync", "scalar", "vector", "gpsimd", "tensor")


class Tok:
    __slots__ = ("sem", "val", "weak")
    def __init__(self, sem, val, weak=False):
        self.sem = sem
        self.val = val
        self.weak = weak


class Prog:
    def __init__(self):
        self.ops = {e: [] for e in ENGS}
        self.count = {e: 0 for e in ENGS}
        self.buf = {}

    def op(self, eng, emit, reads=(), writes=(), extra=(), inc=1, sem=None, weak=False):
        reads = list(reads)
        writes = list(writes)
        for k in list(reads):
            if isinstance(k, tuple) and k and str(k[0]).startswith("pb"):
                reads.remove(k)
                writes.append(k)
        deps = list(extra)
        for k in reads:
            st = self.buf.get(k)
            if st and st[0] is not None:
                deps.append(st[0])
        for k in writes:
            st = self.buf.get(k)
            if st:
                if st[0] is not None:
                    deps.append(st[0])
                for r in st[1]:
                    assert not r.weak, f"WAR on weak (uncovered) PE reader of {k}"
                    deps.append(r)
        sem = sem or eng
        self.count[sem] = self.count.get(sem, 0) + inc
        tok = Tok(sem, self.count[sem], weak)
        self.ops[eng].append((deps, emit, inc, tok))
        for k in reads:
            self.buf.setdefault(k, [None, []])[1].append(tok)
        for k in writes:
            self.buf[k] = [tok, []]
        return tok

    def dma(self, eng, emit, reads=(), writes=(), extra=(), sem=None):
        assert sem is not None
        return self.op(eng, emit, reads, writes, extra, inc=16, sem=sem)

    def cover(self, keys, toks):
        for k in keys:
            st = self.buf.get(k)
            if not st:
                continue
            st[1] = [r for r in st[1] if not r.weak] + list(toks)

    def all_of(self, sem):
        return Tok(sem, self.count.get(sem, 0))

    def emit(self, block, sems):
        for eng in ENGS:
            ops = self.ops[eng]
            if not ops:
                continue

            def make_section(eng, ops):
                def section(h):
                    seen = {}
                    for deps, emit, inc, tok in ops:
                        need = {}
                        for d in deps:
                            if d.val > need.get(d.sem, 0):
                                need[d.sem] = d.val
                        for dsem in sorted(need):
                            dval = need[dsem]
                            if dsem == tok.sem:
                                dval = min(dval, tok.val - inc)
                            if dval <= seen.get(dsem, 0):
                                continue
                            h.wait_ge(sems[dsem], dval)
                            seen[dsem] = dval
                        emit(h).then_inc(sems[tok.sem], inc)
                return section
            getattr(block, eng)(make_section(eng, ops))


def cdiv(a, b):
    return (a + b - 1) // b


C2, C1, CM1, EPSLN, CM05, EPSQ, I128, C0, EPSF, I2048 = range(10)
CONST_VALS = [2.0, 1.0, -1.0, 1e-5, -0.5, 1.1920928955078125e-07,
              1.0 / 128.0, 0.0, 1e-6, 1.0 / 2048.0]


def build_nc(S):
    NCH = S // TC
    nc = bass.Bass("TRN2", target_bir_lowering=False)

    x_in = nc.dram_tensor("x_in", [S, D], F32, kind="ExternalInput").ap()
    win_blk = nc.dram_tensor("win_blk", [NH, NDT, P, 5 * P], F32, kind="ExternalInput").ap()
    wout_blk = nc.dram_tensor("wout_blk", [NDT, P, D], F32, kind="ExternalInput").ap()
    gup_pk = nc.dram_tensor("gup_pk", [GUW, 2 * IO], I32, kind="ExternalInput").ap()
    gup_sc = nc.dram_tensor("gup_sc", [GUW, 2 * IO], F16, kind="ExternalInput").ap()
    dn_pk = nc.dram_tensor("dn_pk", [DNW, D], I32, kind="ExternalInput").ap()
    dn_sc = nc.dram_tensor("dn_sc", [DNW, D], F16, kind="ExternalInput").ap()
    idm_in = nc.dram_tensor("idm", [P, P], BF16, kind="ExternalInput").ap()
    cst_in = nc.dram_tensor("cst", [P, 12], F32, kind="ExternalInput").ap()
    out_hbm = nc.dram_tensor("out", [S, D], F32, kind="ExternalOutput").ap()

    wgu_bf = nc.dram_tensor("wgu_bf", [D, 2 * IO], BF16, kind="Internal").ap()
    wdn_bf = nc.dram_tensor("wdn_bf", [IO, D], BF16, kind="Internal").ap()
    x1_hbm = nc.dram_tensor("x1_hbm", [S, D], F32, kind="Internal").ap()
    win_bfh = nc.dram_tensor("win_bfh", [NH, NDT, P, 5 * P], BF16, kind="Internal").ap()
    wout_bfh = nc.dram_tensor("wout_bfh", [NDT, P, D], BF16, kind="Internal").ap()

    es = ExitStack()
    sb = lambda n, sh, dt: es.enter_context(nc.sbuf_tensor(n, sh, dt))
    ps = lambda n, sh, dt: es.enter_context(nc.psum_tensor(n, sh, dt))

    s_cst = sb("s_cst", [P, 12], F32)
    s_idm = sb("s_idm", [P, P], BF16)
    x_sb = [sb(f"x{t}", [P, D], F32) for t in range(NTT)]
    xn_sb = [sb(f"xn{t}", [P, D], BF16) for t in range(NTT)]
    xnT = [sb(f"xnT{d}", [P, TCA], BF16) for d in range(NDT)]
    win_buf = [sb(f"winb{i}", [P, 4 * 5 * P], BF16) for i in range(NWIN)]
    wsm_buf = [sb(f"wsmb{i}", [P, 4 * TC], BF16) for i in range(4)]
    hy_sb = [sb(f"hy{i}", [P, TC], BF16) for i in range(NIT)]

    def attn_set(s):
        return dict(
            qf=sb(f"a_qf{s}", [P, TCA], F32), vf=sb(f"a_vf{s}", [P, TCA], F32),
            kc=sb(f"a_kc{s}", [P, TCA], F32), as_=sb(f"a_as{s}", [P, TCA], F32),
            gs=sb(f"a_gs{s}", [P, TCA], F32),
            a0=sb(f"a_a0{s}", [P, TCA], F32), a1=sb(f"a_a1{s}", [P, TCA], F32),
            pref=sb(f"a_pref{s}", [P, TCA], F32),
            row=sb(f"s_row{s}", [1, TCA], F32))
    ATT = [attn_set(0)]
    s_ones_c = sb("s_ones_c", [P, 1], BF16)
    s_ones_r = sb("s_ones_r", [1, P], F32)
    s_state = sb("s_state", [P, NH], F32)
    ln_mean = [sb(f"lnm{t}", [P, 1], F32) for t in range(NTT)]
    ln_ssum = [sb(f"lns{t}", [P, 1], F32) for t in range(NTT)]
    ln_m2 = [sb(f"ln2_{t}", [P, 1], F32) for t in range(NTT)]
    ln_var = [sb(f"lnv{t}", [P, 1], F32) for t in range(NTT)]
    ln_rstd = [sb(f"lnr{t}", [P, 1], F32) for t in range(NTT)]
    f_ssum = sb("f_ssum", [P, NTT * 4], F32)
    f_tot = [sb(f"ftot{t}", [P, 1], F32) for t in range(NTT)]
    dq_pk = [sb(f"dq_pk{i}", [P, DQW], I32) for i in range(2)]
    dq_sc = [sb(f"dq_sc{i}", [P, DQW], F16) for i in range(2)]
    dq_em = [sb(f"dq_em{i}", [P, DQW], I32) for i in range(2)]
    dq_sg = [sb(f"dq_sg{i}", [P, DQW], I32) for i in range(2)]
    dq_r1 = [sb(f"dq_r1{i}", [P, DQW], F32) for i in range(2)]
    dq_o = [sb(f"dq_o{i}", [P, DQW], BF16) for i in range(2)]

    pb = [ps(f"pb{i}", [P, TC], F32) for i in range(5)]
    pb_misc = ps("pb_misc", [P, TC], F32)
    pb_tp = [ps(f"pbtp{i}", [P, 1024], BF16) for i in range(2)]

    DMA_SEMS = (["d_cst", "d_idm", "d_dqpk0", "d_dqpk1", "d_dqsc0", "d_dqsc1",
                 "d_wdeqg", "d_wdeqd", "d_x1h", "d_out"] +
                [f"d_x{i}" for i in range(NTT)] +
                [f"d_win{i}" for i in range(NWIN)] +
                [f"d_wsm{i}" for i in range(8)] + ["d_xa0", "d_xa1"] +
                [f"d_wst{i}" for i in range(4)] + [f"d_wot{i}" for i in range(2)])
    sems = {e: es.enter_context(nc.semaphore(f"sem_{e}")) for e in ENGS}
    for snm in DMA_SEMS:
        sems[snm] = es.enter_context(nc.semaphore(snm))
    block = es.enter_context(nc.Block())

    pr = Prog()

    def cc(i, wp=P):
        return s_cst.ap()[0:wp, i:i + 1]

    with es:
        for g in range(4):
            pr.dma("gpsimd", lambda h, g=g: h.dma_start(
                out=win_bfh[g * 4:(g + 1) * 4].rearrange("a b p c -> (a b p) c"),
                in_=win_blk[g * 4:(g + 1) * 4].rearrange("a b p c -> (a b p) c")),
                writes=[("wst", g)], sem=f"d_wst{g}")
        for g in range(2):
            pr.dma("gpsimd", lambda h, g=g: h.dma_start(
                out=wout_bfh[g * 8:(g + 1) * 8].rearrange("a p c -> (a p) c"),
                in_=wout_blk[g * 8:(g + 1) * 8].rearrange("a p c -> (a p) c")),
                writes=[("wot", g)], sem=f"d_wot{g}")
        pr.dma("sync", lambda h: h.dma_start(out=s_cst[:], in_=cst_in),
               writes=[("cst",)], sem="d_cst")
        pr.dma("sync", lambda h: h.dma_start(out=s_idm[:], in_=idm_in),
               writes=[("idm",)], sem="d_idm")
        pr.op("vector", lambda v: v.memset(s_ones_c[:], 1.0), writes=[("onesc",)])
        pr.op("vector", lambda v: v.memset(s_ones_r[:], 1.0), writes=[("onesr",)])
        pr.op("vector", lambda v: v.memset(s_state[:], 0.0),
              writes=[("state", i) for i in range(NH)])

        dq_par = [0, 0]

        def dq_stage_ops(u):
            pk_hbm, sc_hbm, dst, w0, wp, c0, cw, j, pa, ga = u
            kpk, ksc = ("dqpk", ga), ("dqsc", ga)
            kem, ksg, kr1, ko = ("dqem", pa), ("dqsg", pa), ("dqr1", pa), ("dqo", pa)
            em, sg, r1, o, pk, sc = (dq_em[pa], dq_sg[pa], dq_r1[pa], dq_o[pa],
                                     dq_pk[ga], dq_sc[ga])
            wsem = "d_wdeqg" if dst is wgu_bf else "d_wdeqd"

            def s_load():
                if j == 0:
                    pr.dma("sync", lambda h: h.dma_start(
                        out=pk[0:wp, 0:cw], in_=pk_hbm[w0:w0 + wp, c0:c0 + cw]),
                        writes=[kpk], sem=f"d_dqpk{ga}")
                    pr.dma("sync", lambda h: h.dma_start(
                        out=sc[0:wp, 0:cw], in_=sc_hbm[w0:w0 + wp, c0:c0 + cw]),
                        writes=[ksc], sem=f"d_dqsc{ga}")

            def s_trio():
                sh = 22 - 4 * j
                if sh >= 0:
                    pr.op("vector", lambda v: v.tensor_scalar(
                        em[0:wp, 0:cw], pk[0:wp, 0:cw], sh, 7 << 22,
                        OP.logical_shift_left, OP.bitwise_and), reads=[kpk], writes=[kem])
                else:
                    pr.op("vector", lambda v: v.tensor_scalar(
                        em[0:wp, 0:cw], pk[0:wp, 0:cw], -sh, 7 << 22,
                        OP.logical_shift_right, OP.bitwise_and), reads=[kpk], writes=[kem])
                pr.op("vector", lambda v: v.tensor_scalar(
                    sg[0:wp, 0:cw], pk[0:wp, 0:cw], 28 - 4 * j, -2147483648,
                    OP.logical_shift_left, OP.bitwise_and), reads=[kpk], writes=[ksg])
                pr.op("gpsimd", lambda g: g.tensor_scalar(
                    em[0:wp, 0:cw], em[0:wp, 0:cw], 252 << 22, None, OP.add),
                    reads=[kem], writes=[kem])

            def s_relu():
                pr.op("scalar", lambda s: s.activation(
                    r1[0:wp, 0:cw], em.ap()[0:wp, 0:cw].bitcast(F32), AF.Relu,
                    bias=cc(C1, wp), scale=cc(CM1, wp)),
                    reads=[kem, ("cst",)], writes=[kr1])

            def s_vp():
                pr.op("gpsimd", lambda g: g.tensor_tensor(
                    r1[0:wp, 0:cw], em.ap()[0:wp, 0:cw].bitcast(F32), r1[0:wp, 0:cw],
                    OP.subtract), reads=[kem, kr1], writes=[kr1])

            def s_or():
                pr.op("vector", lambda v: v.tensor_tensor(
                    r1.ap()[0:wp, 0:cw].bitcast(I32), r1.ap()[0:wp, 0:cw].bitcast(I32),
                    sg[0:wp, 0:cw], OP.bitwise_or),
                    reads=[kr1, ksg], writes=[kr1])

            def s_sc():
                pr.op("gpsimd", lambda g: g.tensor_tensor(
                    o[0:wp, 0:cw], r1[0:wp, 0:cw], sc[0:wp, 0:cw], OP.mult),
                    reads=[kr1, ksc], writes=[ko])
                pr.dma("sync", lambda h: h.dma_start(
                    out=dst.rearrange("(w e) n -> w e n", e=8)[w0:w0 + wp, j, c0:c0 + cw],
                    in_=o[0:wp, 0:cw]),
                    reads=[ko], writes=[("wdeq", wsem)], sem=wsem)
            return [s_load, s_trio, s_relu, s_vp, s_or, s_sc]

        def emit_unit_pair(ua, ub):
            sa = dq_stage_ops(ua)
            sb = dq_stage_ops(ub) if ub is not None else None
            for i in range(6):
                sa[i]()
                if sb is not None:
                    sb[i]()

        dq_units = []
        NCCP = IO // DQW
        pa_ctr = [0, 0]

        def add_unit(pk_hbm, sc_hbm, dst, w0, wp, c0, cw, j):
            pa = pa_ctr[0] & 1
            pa_ctr[0] += 1
            ga = pa_ctr[1] & 1
            if j == 7:
                pa_ctr[1] += 1
            dq_units.append((pk_hbm, sc_hbm, dst, w0, wp, c0, cw, j, pa, ga))

        gup_need = []
        for g in range(NCCP):
            for ci in (g, NCCP + g):
                c0, cw = ci * DQW, DQW
                for wt in range(cdiv(GUW, P)):
                    w0, wp = wt * P, min(P, GUW - wt * P)
                    for j in range(8):
                        add_unit(gup_pk, gup_sc, wgu_bf, w0, wp, c0, cw, j)
            gup_need.append(len(dq_units))
        n_gup_units = len(dq_units)
        dn_need = []
        for hc in range(4):
            for ci in (2 * hc, 2 * hc + 1):
                c0, cw = ci * DQW, DQW
                for wt in range(cdiv(DNW, P)):
                    w0, wp = wt * P, min(P, DNW - wt * P)
                    for j in range(8):
                        add_unit(dn_pk, dn_sc, wdn_bf, w0, wp, c0, cw, j)
            dn_need.append(len(dq_units))
        dq_pos = [0]

        def emit_dq(n):
            stop = min(len(dq_units), dq_pos[0] + n)
            while dq_pos[0] < stop:
                ua = dq_units[dq_pos[0]]
                ub = dq_units[dq_pos[0] + 1] if dq_pos[0] + 1 < stop else None
                emit_unit_pair(ua, ub)
                dq_pos[0] += 1 if ub is None else 2

        def emit_dq_upto(limit):
            limit = min(limit, len(dq_units))
            while dq_pos[0] < limit:
                ua = dq_units[dq_pos[0]]
                ub = dq_units[dq_pos[0] + 1] if dq_pos[0] + 1 < limit else None
                emit_unit_pair(ua, ub)
                dq_pos[0] += 1 if ub is None else 2

        def ln_pass():
            for tt in range(NTT):
                kx = ("x", tt)
                pr.op("vector", lambda v, tt=tt: v.reduce_sum(
                    ln_mean[tt][:], x_sb[tt][:], axis=AX.X), reads=[kx], writes=[("lnm", tt)])
                pr.op("vector", lambda v, tt=tt: v.tensor_scalar(
                    ln_mean[tt][:], ln_mean[tt][:], 1.0 / D, None, OP.mult),
                    reads=[("lnm", tt)], writes=[("lnm", tt)])
                pr.op("scalar", lambda s, tt=tt: s.activation(
                    xn_sb[tt][:], x_sb[tt][:], AF.Square, bias=cc(C0), scale=cc(C1),
                    accum_out=ln_ssum[tt][:]),
                    reads=[kx, ("cst",)], writes=[("xn", tt), ("lns", tt)])
                pr.op("scalar", lambda s, tt=tt: s.activation(
                    ln_m2[tt][:], ln_mean[tt][:], AF.Square, bias=cc(C0), scale=cc(C1)),
                    reads=[("lnm", tt), ("cst",)], writes=[("ln2", tt)])
                pr.op("vector", lambda v, tt=tt: v.scalar_tensor_tensor(
                    ln_var[tt][:], ln_ssum[tt][:], 1.0 / D, ln_m2[tt][:], OP.mult, OP.subtract),
                    reads=[("lns", tt), ("ln2", tt)], writes=[("lnv", tt)])
                pr.op("scalar", lambda s, tt=tt: s.activation(
                    ln_var[tt][:], ln_var[tt][:], AF.Ln, bias=cc(EPSLN), scale=cc(C1)),
                    reads=[("lnv", tt), ("cst",)], writes=[("lnv", tt)])
                pr.op("scalar", lambda s, tt=tt: s.activation(
                    ln_rstd[tt][:], ln_var[tt][:], AF.Exp, bias=cc(C0), scale=cc(CM05)),
                    reads=[("lnv", tt), ("cst",)], writes=[("lnr", tt)])
                pr.op("vector", lambda v, tt=tt: v.tensor_scalar(
                    xn_sb[tt][:], x_sb[tt][:], ln_mean[tt][:], ln_rstd[tt][:],
                    OP.subtract, OP.mult),
                    reads=[kx, ("lnm", tt), ("lnr", tt)], writes=[("xn", tt)])

        def transpose_pass():
            for tt in range(NTT):
                for dt in range(NDT):
                    par = (tt * NDT + dt) % 2
                    ktp = ("pbtp", par)
                    pr.op("tensor", lambda t, tt=tt, dt=dt, par=par: t.transpose(
                        pb_tp[par].ap()[:, 0:P], xn_sb[tt].ap()[:, dt * P:(dt + 1) * P],
                        s_idm[:]),
                        reads=[("xn", tt), ("idm",), ktp], writes=[ktp], weak=True)
                    t_ev = pr.op("scalar", lambda s, tt=tt, dt=dt, par=par: s.activation(
                        xnT[dt].ap()[:, tt * P:(tt + 1) * P], pb_tp[par].ap()[:, 0:P], AF.Copy),
                        reads=[ktp], writes=[("xnT", dt)])
                    pr.cover([("xn", tt)], [t_ev])

        H5 = 512
        for ch in range(S // TCA):
            base = ch * TCA
            A = ATT[0]
            sx = 0
            for t8 in range(NTTA):
                xb = t8 % 2
                li = t8 % 4
                kx = ("x", xb)
                pr.dma("sync", lambda h, t8=t8, xb=xb, base=base: h.dma_start(
                    out=x_sb[xb][:], in_=x_in[base + t8 * P: base + (t8 + 1) * P, :]),
                    writes=[kx], sem=f"d_x{xb}")
                pr.op("vector", lambda v, xb=xb, li=li: v.reduce_sum(
                    ln_mean[li][:], x_sb[xb][:], axis=AX.X), reads=[kx], writes=[("lnm", li)])
                pr.op("vector", lambda v, li=li: v.tensor_scalar(
                    ln_mean[li][:], ln_mean[li][:], 1.0 / D, None, OP.mult),
                    reads=[("lnm", li)], writes=[("lnm", li)])
                pr.op("scalar", lambda s, xb=xb, li=li: s.activation(
                    xn_sb[xb][:], x_sb[xb][:], AF.Square, bias=cc(C0), scale=cc(C1),
                    accum_out=ln_ssum[li][:]),
                    reads=[kx, ("cst",)], writes=[("xn", xb), ("lns", li)])
                pr.op("scalar", lambda s, li=li: s.activation(
                    ln_m2[li][:], ln_mean[li][:], AF.Square, bias=cc(C0), scale=cc(C1)),
                    reads=[("lnm", li), ("cst",)], writes=[("ln2", li)])
                pr.op("vector", lambda v, li=li: v.scalar_tensor_tensor(
                    ln_var[li][:], ln_ssum[li][:], 1.0 / D, ln_m2[li][:], OP.mult, OP.subtract),
                    reads=[("lns", li), ("ln2", li)], writes=[("lnv", li)])
                pr.op("scalar", lambda s, li=li: s.activation(
                    ln_var[li][:], ln_var[li][:], AF.Ln, bias=cc(EPSLN), scale=cc(C1)),
                    reads=[("lnv", li), ("cst",)], writes=[("lnv", li)])
                pr.op("scalar", lambda s, li=li: s.activation(
                    ln_rstd[li][:], ln_var[li][:], AF.Exp, bias=cc(C0), scale=cc(CM05)),
                    reads=[("lnv", li), ("cst",)], writes=[("lnr", li)])
                pr.op("vector", lambda v, xb=xb, li=li: v.tensor_scalar(
                    xn_sb[xb][:], x_sb[xb][:], ln_mean[li][:], ln_rstd[li][:],
                    OP.subtract, OP.mult),
                    reads=[kx, ("lnm", li), ("lnr", li)], writes=[("xn", xb)])
                for dt in range(NDT):
                    par = dt % 2
                    ktp = ("pbtp", par)
                    pr.op("tensor", lambda t, xb=xb, dt=dt, par=par: t.transpose(
                        pb_tp[par].ap()[:, 0:P], xn_sb[xb].ap()[:, dt * P:(dt + 1) * P],
                        s_idm[:]),
                        reads=[("xn", xb), ("idm",), ktp], writes=[ktp], weak=True)
                    t_ev = pr.op("scalar", lambda s, t8=t8, dt=dt, par=par: s.activation(
                        xnT[dt].ap()[:, t8 * P:(t8 + 1) * P], pb_tp[par].ap()[:, 0:P], AF.Copy),
                        reads=[ktp], writes=[("xnT", dt)])
                    pr.cover([("xn", xb)], [t_ev])
                emit_dq(4)

            def emit_bcast_y(ht):
                for half in range(2):
                    hs = half * H5
                    pr.op("tensor", lambda t, hs=hs: t.matmul(
                        pb_misc[:], s_ones_r[:], A["row"].ap()[0:1, hs:hs + H5],
                        start=True, stop=True),
                        reads=[("rowb", sx), ("onesr",), ("pbm",)], writes=[("pbm",)], weak=True)
                    pr.op("vector", lambda v, hs=hs: v.tensor_tensor(
                        A["kc"].ap()[:, hs:hs + H5], A["kc"].ap()[:, hs:hs + H5],
                        pb_misc[:], OP.mult),
                        reads=[("akc", sx), ("pbm",)], writes=[("akc", sx)])
                    t_y = pr.op("vector", lambda v, ht=ht, hs=hs, half=half: v.tensor_tensor(
                        hy_sb[2 * ht + half][:], A["kc"].ap()[:, hs:hs + H5],
                        A["a1"].ap()[:, hs:hs + H5], OP.add),
                        reads=[("akc", sx), ("aa1", sx)], writes=[("hy", 2 * ht + half)])
                    pr.cover([("rowb", sx), ("onesr",)], [t_y])

            for ht in range(NH):
                for q in range(4):
                    slot = (ht * 4 + q) % NWIN
                    pr.dma("sync", lambda h, ht=ht, q=q, slot=slot: h.dma_start(
                        out=win_buf[slot].ap().rearrange("p (a c) -> p a c", a=4),
                        in_=win_bfh[ht, q * 4:(q + 1) * 4].transpose([1, 0, 2])),
                        reads=[("wst", ht // 4)], writes=[("winb", slot)], sem=f"d_win{slot}")
                ev_first = []
                for half in range(2):
                    hs = half * H5
                    for q in range(4):
                        slot = (ht * 4 + q) % NWIN
                        def g_emit(t, q=q, slot=slot, hs=hs):
                            last = None
                            for dto in range(4):
                                dt = q * 4 + dto
                                coff = dto * 5 * P
                                for pj in range(5):
                                    last = t.matmul(
                                        pb[pj][:],
                                        win_buf[slot].ap()[:, coff + pj * P: coff + (pj + 1) * P],
                                        xnT[dt].ap()[:, hs:hs + H5],
                                        start=(dt == 0), stop=(dt == NDT - 1))
                            return last
                        pr.op("tensor", g_emit,
                              reads=[("winb", slot)] + [("xnT", q * 4 + i) for i in range(4)] +
                                    [("pb", b) for b in range(5)],
                              writes=[("pb", b) for b in range(5)], weak=True)
                    if half == 0 and ht > 0:
                        emit_bcast_y(ht - 1)
                    t_k = pr.op("scalar", lambda s, hs=hs: s.activation(
                        A["kc"].ap()[:, hs:hs + H5], pb[1][:], AF.Tanh, bias=cc(C0), scale=cc(C1)),
                        reads=[("pb", 1), ("cst",)], writes=[("akc", sx)])
                    t_v = pr.op("vector", lambda v, hs=hs: v.tensor_copy(
                        A["vf"].ap()[:, hs:hs + H5], pb[2][:]),
                        reads=[("pb", 2)], writes=[("avf", sx)])
                    t_a = pr.op("scalar", lambda s, hs=hs: s.activation(
                        A["as_"].ap()[:, hs:hs + H5], pb[3][:], AF.Sigmoid, bias=cc(C2), scale=cc(C1)),
                        reads=[("pb", 3), ("cst",)], writes=[("aas", sx)])
                    pr.op("vector", lambda v, hs=hs: v.tensor_scalar(
                        A["as_"].ap()[:, hs:hs + H5], A["as_"].ap()[:, hs:hs + H5],
                        0.6, 0.9995, OP.max, OP.min),
                        reads=[("aas", sx)], writes=[("aas", sx)])
                    t_qf = pr.op("vector", lambda v, hs=hs: v.tensor_copy(
                        A["qf"].ap()[:, hs:hs + H5], pb[0][:]),
                        reads=[("pb", 0)], writes=[("aqf", sx)])
                    pr.op("vector", lambda v, hs=hs, half=half: v.tensor_tensor(
                        hy_sb[32 + half][:], A["qf"].ap()[:, hs:hs + H5],
                        A["qf"].ap()[:, hs:hs + H5], OP.mult),
                        reads=[("aqf", sx)], writes=[("hy", 32 + half)])
                    t_g = pr.op("scalar", lambda s, hs=hs: s.activation(
                        A["gs"].ap()[:, hs:hs + H5], pb[4][:], AF.Sigmoid, bias=cc(C0), scale=cc(C1)),
                        reads=[("pb", 4), ("cst",)], writes=[("ags", sx)])
                    if half == 1:
                        ev_first.append(t_k)
                    pr.op("tensor", lambda t, half=half: t.matmul(
                        pb_misc.ap()[0:1, :], s_ones_c[:], hy_sb[32 + half][:],
                        start=True, stop=True),
                        reads=[("hy", 32 + half), ("onesc",), ("pbm",)], writes=[("pbm",)], weak=True)
                    t_rs = pr.op("vector", lambda v, hs=hs: v.tensor_copy(
                        A["row"].ap()[0:1, hs:hs + H5], pb_misc.ap()[0:1, :]),
                        reads=[("pbm",)], writes=[("rowa", sx)])
                    pr.cover([("hy", 32 + half), ("onesc",)], [t_rs])
                pr.cover([("winb", (ht * 4 + q) % NWIN) for q in range(4)] +
                         [("xnT", d) for d in range(NDT)], ev_first[:1])
                pr.op("scalar", lambda s: s.activation(
                    A["row"][:], A["row"][:], AF.Ln, bias=cc(EPSQ, 1), scale=cc(I128, 1)),
                    reads=[("rowa", sx), ("cst",)], writes=[("rowa", sx)])
                pr.op("scalar", lambda s: s.activation(
                    A["row"][:], A["row"][:], AF.Exp, bias=cc(C0, 1), scale=cc(CM05, 1)),
                    reads=[("rowa", sx), ("cst",)], writes=[("rowb", sx)])
                pr.op("vector", lambda v: v.tensor_copy(
                    A["a0"][:], A["as_"][:]),
                    reads=[("aas", sx)], writes=[("aa0", sx)])
                pr.op("vector", lambda v: v.memset(A["a0"].ap()[:, 0:TCA:SCH], 0.0),
                    reads=[("aa0", sx)], writes=[("aa0", sx)])
                pr.op("vector", lambda v: v.tensor_tensor(
                    A["a1"][:], A["as_"][:], A["a0"][:], OP.subtract),
                    reads=[("aas", sx), ("aa0", sx)], writes=[("aa1", sx)])
                pr.op("vector", lambda v: v.tensor_tensor_scan(
                    A["pref"][:], A["a0"][:], A["a1"][:], 1.0, OP.mult, OP.add),
                    reads=[("aa0", sx), ("aa1", sx)], writes=[("apref", sx)])
                pr.op("vector", lambda v: v.tensor_scalar(
                    A["pref"][:], A["pref"][:], 1e6, 1.0, OP.mult, OP.min),
                    reads=[("apref", sx)], writes=[("apref", sx)])
                pr.op("vector", lambda v: v.tensor_tensor(
                    A["a1"][:], A["kc"][:], A["vf"][:], OP.mult),
                    reads=[("akc", sx), ("avf", sx)], writes=[("aa1", sx)])
                pr.op("vector", lambda v: v.tensor_tensor(
                    A["pref"][:], A["a1"][:], A["pref"][:], OP.mult),
                    reads=[("aa1", sx), ("apref", sx)], writes=[("apref", sx)])
                if ch == 0:
                    pr.op("vector", lambda v: v.tensor_tensor_scan(
                        A["a0"][:], A["as_"][:], A["pref"][:], 0.0, OP.mult, OP.add),
                        reads=[("aas", sx), ("apref", sx)], writes=[("aa0", sx)])
                else:
                    pr.op("vector", lambda v, ht=ht: v.tensor_tensor_scan(
                        A["a0"][:], A["as_"][:], A["pref"][:], s_state.ap()[:, ht:ht + 1],
                        OP.mult, OP.add),
                        reads=[("aas", sx), ("apref", sx), ("state", ht)], writes=[("aa0", sx)])
                if ch < S // TCA - 1:
                    pr.op("vector", lambda v, ht=ht: v.tensor_copy(
                        s_state.ap()[:, ht:ht + 1], A["a0"].ap()[:, TCA - 1:TCA]),
                        reads=[("aa0", sx)], writes=[("state", ht)])
                pr.op("vector", lambda v: v.tensor_tensor(
                    A["kc"][:], A["qf"][:], A["a0"][:], OP.mult),
                    reads=[("aqf", sx), ("aa0", sx)], writes=[("akc", sx)])
                pr.op("vector", lambda v: v.tensor_tensor(
                    A["kc"][:], A["kc"][:], A["gs"][:], OP.mult),
                    reads=[("akc", sx), ("ags", sx)], writes=[("akc", sx)])
                pr.op("vector", lambda v: v.tensor_tensor(
                    A["a1"][:], A["gs"][:], A["vf"][:], OP.mult),
                    reads=[("ags", sx), ("avf", sx)], writes=[("aa1", sx)])
                pr.op("vector", lambda v: v.tensor_tensor(
                    A["a1"][:], A["vf"][:], A["a1"][:], OP.subtract),
                    reads=[("avf", sx), ("aa1", sx)], writes=[("aa1", sx)])
                emit_dq(12)
            emit_bcast_y(NH - 1)

            for thalf in range(2):
                for oc in range(4):
                    for cg in range(4):
                        pr.dma("sync", lambda h, oc=oc, cg=cg: h.dma_start(
                            out=wsm_buf[cg].ap().rearrange("p (a c) -> p a c", a=4),
                            in_=wout_bfh[cg * 4:(cg + 1) * 4, :, oc * TC:(oc + 1) * TC].transpose([1, 0, 2])),
                            reads=[("wot", cg // 2)], writes=[("wsmr", 2 * cg), ("wsmr", 2 * cg + 1)],
                            sem=f"d_wsm{cg}")
                    def wo_emit(t, thalf=thalf, oc=oc):
                        last = None
                        for ct in range(NDT):
                            for tt in range(NTT):
                                tg = thalf * H5 + tt * P
                                last = t.matmul(
                                    pb[tt][:],
                                    hy_sb[2 * ct + tg // H5].ap()[:, (tg % H5):(tg % H5) + P],
                                    wsm_buf[ct // 4].ap()[:, (ct % 4) * TC:(ct % 4 + 1) * TC],
                                    start=(ct == 0), stop=(ct == NDT - 1))
                        return last
                    pr.op("tensor", wo_emit,
                          reads=[("hy", c2) for c2 in range(2 * NDT)] +
                                [("wsmr", r) for r in range(8)] +
                                [("pb", tt) for tt in range(NTT)],
                          writes=[("pb", tt) for tt in range(NTT)], weak=True)
                    ev = []
                    for tt in range(NTT):
                        sl = tt % 2
                        t_cp = pr.op("vector", lambda v, tt=tt, sl=sl: v.tensor_copy(
                            A["pref"].ap()[:, sl * H5: sl * H5 + H5], pb[tt][:]),
                            reads=[("pb", tt)], writes=[("apref", sl, "wo")])
                        ev.append(t_cp)
                        pr.dma("sync", lambda h, thalf=thalf, oc=oc, tt=tt, sl=sl, base=base: h.dma_start(
                            out=x1_hbm[base + thalf * H5 + tt * P: base + thalf * H5 + (tt + 1) * P,
                                       oc * TC:(oc + 1) * TC],
                            in_=A["pref"].ap()[:, sl * H5: sl * H5 + H5]),
                            reads=[("apref", sl, "wo")], writes=[("x1h", ch, thalf, oc, tt)],
                            sem="d_x1h")
                    pr.cover([("hy", c2) for c2 in range(2 * NDT)] +
                             [("wsmr", r) for r in range(8)], ev)
                    emit_dq(6)

        wgu_r = wgu_bf.rearrange("(dt p) (r io) -> dt p r io", p=P, io=IO)
        wdn_r = wdn_bf.rearrange("(it p) n -> it p n", p=P)
        for ch in range(NCH):
            base = ch * TC
            for tt in range(NTT):
                pr.dma("sync", lambda h, tt=tt, base=base: h.dma_start(
                    out=x_sb[tt][:], in_=x1_hbm[base + tt * P: base + (tt + 1) * P, :]),
                    writes=[("x", tt)],
                    extra=[pr.all_of("d_x1h"), pr.all_of("d_out")], sem=f"d_x{tt}")
                pr.dma("sync", lambda h, tt=tt, base=base: h.dma_start(
                    out=ATT[0]["a0"][:], in_=x_in[base + tt * P: base + (tt + 1) * P, 0:TCA]),
                    reads=[("aa0", 0)], writes=[("aa0", 0)], sem="d_xa0")
                pr.dma("sync", lambda h, tt=tt, base=base: h.dma_start(
                    out=ATT[0]["pref"][:], in_=x_in[base + tt * P: base + (tt + 1) * P, TCA:D]),
                    reads=[("apref", 0)], writes=[("apref", 0)], sem="d_xa1")
                pr.op("vector", lambda v, tt=tt: v.tensor_tensor(
                    x_sb[tt].ap()[:, 0:TCA], x_sb[tt].ap()[:, 0:TCA], ATT[0]["a0"][:], OP.add),
                    reads=[("x", tt), ("aa0", 0)], writes=[("x", tt)])
                pr.op("vector", lambda v, tt=tt: v.tensor_tensor(
                    x_sb[tt].ap()[:, TCA:D], x_sb[tt].ap()[:, TCA:D], ATT[0]["pref"][:], OP.add),
                    reads=[("x", tt), ("apref", 0)], writes=[("x", tt)])
            ln_pass()
            transpose_pass()
            emit_dq(20)

            for pair in range(NPAIR):
                emit_dq_upto(gup_need[min(pair // 2 + 1, NCCP - 1)])
                pbg = pb[0] if pair % 2 == 0 else pb[2]
                pbu = pb[1] if pair % 2 == 0 else pb[3]
                kg = ("pb", 0 if pair % 2 == 0 else 2)
                ku = ("pb", 1 if pair % 2 == 0 else 3)
                for q in range(4):
                    vs = (pair * 4 + q) % 8
                    tb, vo = vs % 4, (vs // 4) * 1024
                    extraw = [("winb", vs % 4)] if pair * 4 + q in (4, 5, 6, 7) else []
                    pr.dma("sync", lambda h, pair=pair, q=q, tb=tb, vo=vo: h.dma_start(
                        out=win_buf[tb].ap()[:, vo:vo + 512].rearrange("p (a c) -> p a c", a=4),
                        in_=wgu_r[q * 4:(q + 1) * 4, :, 0, pair * P:(pair + 1) * P].transpose(
                            [1, 0, 2])),
                        reads=[("wdeq", "d_wdeqg")], writes=[("winb", 8 + vs)] + extraw,
                        sem=f"d_win{tb}")
                    pr.dma("sync", lambda h, pair=pair, q=q, tb=tb, vo=vo: h.dma_start(
                        out=win_buf[tb].ap()[:, vo + 512:vo + 1024].rearrange("p (a c) -> p a c", a=4),
                        in_=wgu_r[q * 4:(q + 1) * 4, :, 1, pair * P:(pair + 1) * P].transpose(
                            [1, 0, 2])),
                        reads=[("wdeq", "d_wdeqg")], writes=[("winb", 8 + vs)],
                        sem=f"d_win{tb}")
                for q in range(4):
                    vs = (pair * 4 + q) % 8
                    tb, vo = vs % 4, (vs // 4) * 1024
                    def gu_mm(t, q=q, tb=tb, vo=vo, pbg=pbg, pbu=pbu):
                        last = None
                        for dto in range(4):
                            dt = q * 4 + dto
                            last = t.matmul(pbg[:],
                                            win_buf[tb].ap()[:, vo + dto * P:vo + (dto + 1) * P],
                                            xnT[dt].ap()[:, 0:TC], start=(dt == 0), stop=(dt == NDT - 1))
                            last = t.matmul(pbu[:],
                                            win_buf[tb].ap()[:, vo + 512 + dto * P: vo + 512 + (dto + 1) * P],
                                            xnT[dt].ap()[:, 0:TC], start=(dt == 0), stop=(dt == NDT - 1))
                        return last
                    pr.op("tensor", gu_mm,
                          reads=[("winb", 8 + vs)] + [("xnT", q * 4 + i) for i in range(4)] +
                                [kg, ku],
                          writes=[kg, ku], weak=True)
                t_si = pr.op("scalar", lambda s, pair=pair, pbg=pbg: s.activation(
                    hy_sb[pair][:], pbg[:], AF.Silu, bias=cc(C0), scale=cc(C1)),
                    reads=[kg, ("cst",)], writes=[("hy", pair)])
                t_h = pr.op("vector", lambda v, pair=pair, pbu=pbu: v.tensor_tensor(
                    hy_sb[pair][:], hy_sb[pair][:], pbu[:], OP.mult),
                    reads=[("hy", pair), ku], writes=[("hy", pair)])
                pr.cover([("winb", 8 + (pair * 4 + q) % 8) for q in range(4)] +
                         [("xnT", d) for d in range(NDT)], [t_si, t_h])
                if pair >= NPAIR - 12:
                    emit_dq(max(1, cdiv(len(dq_units) - n_gup_units, 10)))

            for hc in range(4):
                emit_dq_upto(dn_need[hc])
                since = set()
                for ip in range(NIT // 2):
                    reg = ip % 8
                    tb, half = reg % 4, reg // 4
                    pr.dma("sync", lambda h, hc=hc, ip=ip, tb=tb, half=half: h.dma_start(
                        out=wsm_buf[tb].ap()[:, half * 2 * TC: (half + 1) * 2 * TC].rearrange(
                            "p (a c) -> p a c", a=2),
                        in_=wdn_r[2 * ip:2 * ip + 2, :, hc * TC:(hc + 1) * TC].transpose([1, 0, 2])),
                        reads=[("wdeq", "d_wdeqd")], writes=[("wsmr", reg)], sem=f"d_wsm{tb}")
                    def dn_mm(t, ip=ip, tb=tb, half=half):
                        last = None
                        for k in range(2):
                            it = 2 * ip + k
                            for tt in range(NTT):
                                last = t.matmul(
                                    pb[tt][:],
                                    hy_sb[it].ap()[:, tt * P:(tt + 1) * P],
                                    wsm_buf[tb].ap()[:, (half * 2 + k) * TC:(half * 2 + k + 1) * TC],
                                    start=(it == 0), stop=(it == NIT - 1))
                        return last
                    pr.op("tensor", dn_mm,
                          reads=[("hy", 2 * ip), ("hy", 2 * ip + 1), ("wsmr", reg)] +
                                [("pb", tt) for tt in range(NTT)],
                          writes=[("pb", tt) for tt in range(NTT)], weak=True)
                    since.add(reg)
                    if (ip + 1) % 8 == 0 and ip != NIT // 2 - 1:
                        t_dr = pr.op("tensor", lambda t: t.drain(fusable=False))
                        pr.cover([("wsmr", r) for r in since], [t_dr])
                        since = set()
                ev = []
                for tt in range(NTT):
                    t_cp = pr.op("vector", lambda v, hc=hc, tt=tt: v.tensor_copy(
                        xn_sb[tt].ap()[:, hc * TC:(hc + 1) * TC], pb[tt][:]),
                        reads=[("pb", tt)], writes=[("xn", tt)])
                    t_sq = pr.op("scalar", lambda s, hc=hc, tt=tt: s.activation(
                        ATT[0]["a0"].ap()[:, 0:TC], pb[tt][:], AF.Square, bias=cc(C0), scale=cc(C1),
                        accum_out=f_ssum.ap()[:, tt * 4 + hc: tt * 4 + hc + 1]),
                        reads=[("pb", tt), ("cst",)],
                        writes=[("aa0", 0), ("fss", tt, hc)])
                    ev += [t_cp, t_sq]
                pr.cover([("hy", it) for it in range(NIT)] +
                         [("wsmr", r) for r in range(8)], ev)
            for tt in range(NTT):
                pr.op("vector", lambda v, tt=tt: v.reduce_sum(
                    f_tot[tt][:], f_ssum.ap()[:, tt * 4:(tt + 1) * 4], axis=AX.X),
                    reads=[("fss", tt, h2) for h2 in range(4)], writes=[("ftot", tt)])
                pr.op("scalar", lambda s, tt=tt: s.activation(
                    f_tot[tt][:], f_tot[tt][:], AF.Ln, bias=cc(EPSF), scale=cc(I2048)),
                    reads=[("ftot", tt), ("cst",)], writes=[("ftot", tt)])
                pr.op("scalar", lambda s, tt=tt: s.activation(
                    f_tot[tt][:], f_tot[tt][:], AF.Exp, bias=cc(C0), scale=cc(CM05)),
                    reads=[("ftot", tt), ("cst",)], writes=[("ftot", tt)])
                pr.op("vector", lambda v, tt=tt: v.scalar_tensor_tensor(
                    x_sb[tt][:], xn_sb[tt][:], f_tot[tt][:], x_sb[tt][:], OP.mult, OP.add),
                    reads=[("xn", tt), ("ftot", tt), ("x", tt)], writes=[("x", tt)])
                pr.dma("sync", lambda h, tt=tt, base=base: h.dma_start(
                    out=out_hbm[base + tt * P: base + (tt + 1) * P, :], in_=x_sb[tt][:]),
                    reads=[("x", tt)], writes=[("outh", ch, tt)], sem="d_out")

        pr.op("sync", lambda h: h.nop(), extra=[pr.all_of("d_out")])
        pr.emit(block, sems)
    return nc


def _host_layouts(w_in, w_out, gup_packed, gup_scales, dn_packed, dn_scales):
    wT = np.ascontiguousarray(w_in.T)
    win_blk = np.ascontiguousarray(
        wT.reshape(NDT, P, 5, NH, P).transpose(3, 0, 1, 2, 4).reshape(NH, NDT, P, 5 * P))
    wout_blk = np.ascontiguousarray(w_out.T.reshape(NDT, P, D))
    gup_pk = np.ascontiguousarray(gup_packed.T)
    gup_sc = np.ascontiguousarray(np.repeat(gup_scales, 16, axis=0))
    dn_pk = np.ascontiguousarray(dn_packed.T)
    dn_sc = np.ascontiguousarray(np.repeat(dn_scales, 16, axis=0))
    idm = np.eye(P, dtype=np.float32).astype(ml_dtypes.bfloat16)
    cst = np.broadcast_to(np.array(CONST_VALS + [0.0, 0.0], np.float32), (P, 12)).copy()
    return win_blk, wout_blk, gup_pk, gup_sc, dn_pk, dn_sc, idm, cst


def kernel(L0_attn_x, L0_attn_w_in, L0_attn_w_out, L0_ffn_gate_up_packed,
           L0_ffn_gate_up_scales, L0_ffn_down_packed, L0_ffn_down_scales):
    x = np.ascontiguousarray(np.asarray(L0_attn_x, dtype=np.float32))
    B, S, Dm = x.shape
    assert Dm == D
    win_blk, wout_blk, gup_pk, gup_sc, dn_pk, dn_sc, idm, cst = _host_layouts(
        np.asarray(L0_attn_w_in, np.float32), np.asarray(L0_attn_w_out, np.float32),
        np.asarray(L0_ffn_gate_up_packed, np.int32),
        np.asarray(L0_ffn_gate_up_scales, np.float16),
        np.asarray(L0_ffn_down_packed, np.int32),
        np.asarray(L0_ffn_down_scales, np.float16))
    nc = build_nc(S)
    shared = dict(win_blk=win_blk, wout_blk=wout_blk, gup_pk=gup_pk, gup_sc=gup_sc,
                  dn_pk=dn_pk, dn_sc=dn_sc, idm=idm, cst=cst)
    in_maps = [dict(x_in=np.ascontiguousarray(x[c]), **shared) for c in range(B)]
    res = run_bass_kernel_spmd(nc, in_maps, core_ids=list(range(B)))
    out = np.stack([np.asarray(res.results[c]["out"], dtype=np.float32) for c in range(B)])
    return out


# version 12
# speedup vs baseline: 1.0049x; 1.0049x over previous
import numpy as np
import ml_dtypes
import concourse.bass as bass
import concourse.mybir as mybir
from concourse.bass_utils import run_bass_kernel_spmd
from contextlib import ExitStack

F32 = mybir.dt.float32
F16 = mybir.dt.float16
BF16 = mybir.dt.bfloat16
I32 = mybir.dt.int32
AF = mybir.ActivationFunctionType
OP = mybir.AluOpType
AX = mybir.AxisListType

P = 128
D = 2048
NDT = D // P
NH = 16
IO = 5632
NIT = IO // P
NPAIR = IO // P
SCH = 64
TC = 512
NTT = TC // P
GUW = D // 8
DNW = IO // 8
DQW = 256
TCA = 1024
NTTA = TCA // P
NWIN = 4

ENGS = ("sync", "scalar", "vector", "gpsimd", "tensor")


class Tok:
    __slots__ = ("sem", "val", "weak")
    def __init__(self, sem, val, weak=False):
        self.sem = sem
        self.val = val
        self.weak = weak


class Prog:
    def __init__(self):
        self.ops = {e: [] for e in ENGS}
        self.count = {e: 0 for e in ENGS}
        self.buf = {}

    def op(self, eng, emit, reads=(), writes=(), extra=(), inc=1, sem=None, weak=False):
        reads = list(reads)
        writes = list(writes)
        for k in list(reads):
            if isinstance(k, tuple) and k and str(k[0]).startswith("pb"):
                reads.remove(k)
                writes.append(k)
        deps = list(extra)
        for k in reads:
            st = self.buf.get(k)
            if st and st[0] is not None:
                deps.append(st[0])
        for k in writes:
            st = self.buf.get(k)
            if st:
                if st[0] is not None:
                    deps.append(st[0])
                for r in st[1]:
                    assert not r.weak, f"WAR on weak (uncovered) PE reader of {k}"
                    deps.append(r)
        sem = sem or eng
        self.count[sem] = self.count.get(sem, 0) + inc
        tok = Tok(sem, self.count[sem], weak)
        self.ops[eng].append((deps, emit, inc, tok))
        for k in reads:
            self.buf.setdefault(k, [None, []])[1].append(tok)
        for k in writes:
            self.buf[k] = [tok, []]
        return tok

    def dma(self, eng, emit, reads=(), writes=(), extra=(), sem=None):
        assert sem is not None
        return self.op(eng, emit, reads, writes, extra, inc=16, sem=sem)

    def cover(self, keys, toks):
        for k in keys:
            st = self.buf.get(k)
            if not st:
                continue
            st[1] = [r for r in st[1] if not r.weak] + list(toks)

    def all_of(self, sem):
        return Tok(sem, self.count.get(sem, 0))

    def emit(self, block, sems):
        for eng in ENGS:
            ops = self.ops[eng]
            if not ops:
                continue

            def make_section(eng, ops):
                def section(h):
                    seen = {}
                    for deps, emit, inc, tok in ops:
                        need = {}
                        for d in deps:
                            if d.val > need.get(d.sem, 0):
                                need[d.sem] = d.val
                        for dsem in sorted(need):
                            dval = need[dsem]
                            if dsem == tok.sem:
                                dval = min(dval, tok.val - inc)
                            if dval <= seen.get(dsem, 0):
                                continue
                            h.wait_ge(sems[dsem], dval)
                            seen[dsem] = dval
                        emit(h).then_inc(sems[tok.sem], inc)
                return section
            getattr(block, eng)(make_section(eng, ops))


def cdiv(a, b):
    return (a + b - 1) // b


C2, C1, CM1, EPSLN, CM05, EPSQ, I128, C0, EPSF, I2048 = range(10)
CONST_VALS = [2.0, 1.0, -1.0, 1e-5, -0.5, 1.1920928955078125e-07,
              1.0 / 128.0, 0.0, 1e-6, 1.0 / 2048.0]


def build_nc(S):
    NCH = S // TC
    nc = bass.Bass("TRN2", target_bir_lowering=False)

    x_in = nc.dram_tensor("x_in", [S, D], F32, kind="ExternalInput").ap()
    win_blk = nc.dram_tensor("win_blk", [NH, NDT, P, 5 * P], F32, kind="ExternalInput").ap()
    wout_blk = nc.dram_tensor("wout_blk", [NDT, P, D], F32, kind="ExternalInput").ap()
    gup_pk = nc.dram_tensor("gup_pk", [GUW, 2 * IO], I32, kind="ExternalInput").ap()
    gup_sc = nc.dram_tensor("gup_sc", [GUW, 2 * IO], F16, kind="ExternalInput").ap()
    dn_pk = nc.dram_tensor("dn_pk", [DNW, D], I32, kind="ExternalInput").ap()
    dn_sc = nc.dram_tensor("dn_sc", [DNW, D], F16, kind="ExternalInput").ap()
    idm_in = nc.dram_tensor("idm", [P, P], BF16, kind="ExternalInput").ap()
    cst_in = nc.dram_tensor("cst", [P, 12], F32, kind="ExternalInput").ap()
    out_hbm = nc.dram_tensor("out", [S, D], F32, kind="ExternalOutput").ap()

    wgu_bf = nc.dram_tensor("wgu_bf", [D, 2 * IO], BF16, kind="Internal").ap()
    wdn_bf = nc.dram_tensor("wdn_bf", [IO, D], BF16, kind="Internal").ap()
    x1_hbm = nc.dram_tensor("x1_hbm", [S, D], F32, kind="Internal").ap()
    win_bfh = nc.dram_tensor("win_bfh", [NH, NDT, P, 5 * P], BF16, kind="Internal").ap()
    wout_bfh = nc.dram_tensor("wout_bfh", [NDT, P, D], BF16, kind="Internal").ap()

    es = ExitStack()
    sb = lambda n, sh, dt: es.enter_context(nc.sbuf_tensor(n, sh, dt))
    ps = lambda n, sh, dt: es.enter_context(nc.psum_tensor(n, sh, dt))

    s_cst = sb("s_cst", [P, 12], F32)
    s_idm = sb("s_idm", [P, P], BF16)
    x_sb = [sb(f"x{t}", [P, D], F32) for t in range(NTT)]
    xn_sb = [sb(f"xn{t}", [P, D], BF16) for t in range(NTT)]
    xnT = [sb(f"xnT{d}", [P, TCA], BF16) for d in range(NDT)]
    win_buf = [sb(f"winb{i}", [P, 4 * 5 * P], BF16) for i in range(NWIN)]
    wsm_buf = [sb(f"wsmb{i}", [P, 4 * TC], BF16) for i in range(4)]
    hy_sb = [sb(f"hy{i}", [P, TC], BF16) for i in range(NIT)]

    def attn_set(s):
        return dict(
            qf=sb(f"a_qf{s}", [P, TCA], F32), vf=sb(f"a_vf{s}", [P, TCA], F32),
            kc=sb(f"a_kc{s}", [P, TCA], F32), as_=sb(f"a_as{s}", [P, TCA], F32),
            gs=sb(f"a_gs{s}", [P, TCA], F32),
            a0=sb(f"a_a0{s}", [P, TCA], F32), a1=sb(f"a_a1{s}", [P, TCA], F32),
            pref=sb(f"a_pref{s}", [P, TCA], F32),
            row=sb(f"s_row{s}", [1, TCA], F32))
    ATT = [attn_set(0)]
    s_ones_c = sb("s_ones_c", [P, 1], BF16)
    s_ones_r = sb("s_ones_r", [1, P], F32)
    s_state = sb("s_state", [P, NH], F32)
    ln_mean = [sb(f"lnm{t}", [P, 1], F32) for t in range(NTT)]
    ln_ssum = [sb(f"lns{t}", [P, 1], F32) for t in range(NTT)]
    ln_m2 = [sb(f"ln2_{t}", [P, 1], F32) for t in range(NTT)]
    ln_var = [sb(f"lnv{t}", [P, 1], F32) for t in range(NTT)]
    ln_rstd = [sb(f"lnr{t}", [P, 1], F32) for t in range(NTT)]
    f_ssum = sb("f_ssum", [P, NTT * 4], F32)
    f_tot = [sb(f"ftot{t}", [P, 1], F32) for t in range(NTT)]
    dq_pk = [sb(f"dq_pk{i}", [P, DQW], I32) for i in range(2)]
    dq_sc = [sb(f"dq_sc{i}", [P, DQW], F16) for i in range(2)]
    dq_em = [sb(f"dq_em{i}", [P, DQW], I32) for i in range(2)]
    dq_sg = [sb(f"dq_sg{i}", [P, DQW], I32) for i in range(2)]
    dq_r1 = [sb(f"dq_r1{i}", [P, DQW], F32) for i in range(2)]
    dq_o = [sb(f"dq_o{i}", [P, DQW], BF16) for i in range(2)]

    pb = [ps(f"pb{i}", [P, TC], F32) for i in range(5)]
    pb_misc = ps("pb_misc", [P, TC], F32)
    pb_tp = [ps(f"pbtp{i}", [P, 1024], BF16) for i in range(2)]

    DMA_SEMS = (["d_cst", "d_idm", "d_dqpk0", "d_dqpk1", "d_dqsc0", "d_dqsc1",
                 "d_wdeqg", "d_wdeqd", "d_x1h", "d_out"] +
                [f"d_x{i}" for i in range(NTT)] +
                [f"d_win{i}" for i in range(NWIN)] +
                [f"d_wsm{i}" for i in range(8)] + ["d_xa0", "d_xa1"] +
                [f"d_wst{i}" for i in range(4)] + [f"d_wot{i}" for i in range(2)])
    sems = {e: es.enter_context(nc.semaphore(f"sem_{e}")) for e in ENGS}
    for snm in DMA_SEMS:
        sems[snm] = es.enter_context(nc.semaphore(snm))
    block = es.enter_context(nc.Block())

    pr = Prog()

    def cc(i, wp=P):
        return s_cst.ap()[0:wp, i:i + 1]

    with es:
        for g in range(4):
            pr.dma("gpsimd", lambda h, g=g: h.dma_start(
                out=win_bfh[g * 4:(g + 1) * 4].rearrange("a b p c -> (a b p) c"),
                in_=win_blk[g * 4:(g + 1) * 4].rearrange("a b p c -> (a b p) c")),
                writes=[("wst", g)], sem=f"d_wst{g}")
        for g in range(2):
            pr.dma("gpsimd", lambda h, g=g: h.dma_start(
                out=wout_bfh[g * 8:(g + 1) * 8].rearrange("a p c -> (a p) c"),
                in_=wout_blk[g * 8:(g + 1) * 8].rearrange("a p c -> (a p) c")),
                writes=[("wot", g)], sem=f"d_wot{g}")
        pr.dma("sync", lambda h: h.dma_start(out=s_cst[:], in_=cst_in),
               writes=[("cst",)], sem="d_cst")
        pr.dma("sync", lambda h: h.dma_start(out=s_idm[:], in_=idm_in),
               writes=[("idm",)], sem="d_idm")
        pr.op("vector", lambda v: v.memset(s_ones_c[:], 1.0), writes=[("onesc",)])
        pr.op("vector", lambda v: v.memset(s_ones_r[:], 1.0), writes=[("onesr",)])
        pr.op("vector", lambda v: v.memset(s_state[:], 0.0),
              writes=[("state", i) for i in range(NH)])

        dq_par = [0, 0]

        def dq_stage_ops(u):
            pk_hbm, sc_hbm, dst, w0, wp, c0, cw, j, pa, ga = u
            kpk, ksc = ("dqpk", ga), ("dqsc", ga)
            kem, ksg, kr1, ko = ("dqem", pa), ("dqsg", pa), ("dqr1", pa), ("dqo", pa)
            em, sg, r1, o, pk, sc = (dq_em[pa], dq_sg[pa], dq_r1[pa], dq_o[pa],
                                     dq_pk[ga], dq_sc[ga])
            wsem = "d_wdeqg" if dst is wgu_bf else "d_wdeqd"

            def s_load():
                if j == 0:
                    pr.dma("sync", lambda h: h.dma_start(
                        out=pk[0:wp, 0:cw], in_=pk_hbm[w0:w0 + wp, c0:c0 + cw]),
                        writes=[kpk], sem=f"d_dqpk{ga}")
                    pr.dma("sync", lambda h: h.dma_start(
                        out=sc[0:wp, 0:cw], in_=sc_hbm[w0:w0 + wp, c0:c0 + cw]),
                        writes=[ksc], sem=f"d_dqsc{ga}")

            def s_trio():
                sh = 22 - 4 * j
                if sh >= 0:
                    pr.op("vector", lambda v: v.tensor_scalar(
                        em[0:wp, 0:cw], pk[0:wp, 0:cw], sh, 7 << 22,
                        OP.logical_shift_left, OP.bitwise_and), reads=[kpk], writes=[kem])
                else:
                    pr.op("vector", lambda v: v.tensor_scalar(
                        em[0:wp, 0:cw], pk[0:wp, 0:cw], -sh, 7 << 22,
                        OP.logical_shift_right, OP.bitwise_and), reads=[kpk], writes=[kem])
                pr.op("vector", lambda v: v.tensor_scalar(
                    sg[0:wp, 0:cw], pk[0:wp, 0:cw], 28 - 4 * j, -2147483648,
                    OP.logical_shift_left, OP.bitwise_and), reads=[kpk], writes=[ksg])
                pr.op("gpsimd", lambda g: g.tensor_scalar(
                    em[0:wp, 0:cw], em[0:wp, 0:cw], 252 << 22, None, OP.add),
                    reads=[kem], writes=[kem])

            def s_relu():
                pr.op("scalar", lambda s: s.activation(
                    r1[0:wp, 0:cw], em.ap()[0:wp, 0:cw].bitcast(F32), AF.Relu,
                    bias=cc(C1, wp), scale=cc(CM1, wp)),
                    reads=[kem, ("cst",)], writes=[kr1])

            def s_vp():
                pr.op("gpsimd", lambda g: g.tensor_tensor(
                    r1[0:wp, 0:cw], em.ap()[0:wp, 0:cw].bitcast(F32), r1[0:wp, 0:cw],
                    OP.subtract), reads=[kem, kr1], writes=[kr1])

            def s_or():
                pr.op("vector", lambda v: v.tensor_tensor(
                    r1.ap()[0:wp, 0:cw].bitcast(I32), r1.ap()[0:wp, 0:cw].bitcast(I32),
                    sg[0:wp, 0:cw], OP.bitwise_or),
                    reads=[kr1, ksg], writes=[kr1])

            def s_sc():
                pr.op("gpsimd", lambda g: g.tensor_tensor(
                    o[0:wp, 0:cw], r1[0:wp, 0:cw], sc[0:wp, 0:cw], OP.mult),
                    reads=[kr1, ksc], writes=[ko])
                pr.dma("sync", lambda h: h.dma_start(
                    out=dst.rearrange("(w e) n -> w e n", e=8)[w0:w0 + wp, j, c0:c0 + cw],
                    in_=o[0:wp, 0:cw]),
                    reads=[ko], writes=[("wdeq", wsem)], sem=wsem)
            return [s_load, s_trio, s_relu, s_vp, s_or, s_sc]

        def emit_unit_pair(ua, ub):
            sa = dq_stage_ops(ua)
            sb = dq_stage_ops(ub) if ub is not None else None
            for i in range(6):
                sa[i]()
                if sb is not None:
                    sb[i]()

        dq_units = []
        NCCP = IO // DQW
        pa_ctr = [0, 0]

        def add_unit(pk_hbm, sc_hbm, dst, w0, wp, c0, cw, j):
            pa = pa_ctr[0] & 1
            pa_ctr[0] += 1
            ga = pa_ctr[1] & 1
            if j == 7:
                pa_ctr[1] += 1
            dq_units.append((pk_hbm, sc_hbm, dst, w0, wp, c0, cw, j, pa, ga))

        gup_need = []
        for g in range(NCCP):
            for ci in (g, NCCP + g):
                c0, cw = ci * DQW, DQW
                for wt in range(cdiv(GUW, P)):
                    w0, wp = wt * P, min(P, GUW - wt * P)
                    for j in range(8):
                        add_unit(gup_pk, gup_sc, wgu_bf, w0, wp, c0, cw, j)
            gup_need.append(len(dq_units))
        n_gup_units = len(dq_units)
        dn_need = []
        for hc in range(4):
            for ci in (2 * hc, 2 * hc + 1):
                c0, cw = ci * DQW, DQW
                for wt in range(cdiv(DNW, P)):
                    w0, wp = wt * P, min(P, DNW - wt * P)
                    for j in range(8):
                        add_unit(dn_pk, dn_sc, wdn_bf, w0, wp, c0, cw, j)
            dn_need.append(len(dq_units))
        dq_pos = [0]

        def emit_dq(n):
            stop = min(len(dq_units), dq_pos[0] + n)
            while dq_pos[0] < stop:
                ua = dq_units[dq_pos[0]]
                ub = dq_units[dq_pos[0] + 1] if dq_pos[0] + 1 < stop else None
                emit_unit_pair(ua, ub)
                dq_pos[0] += 1 if ub is None else 2

        def emit_dq_upto(limit):
            limit = min(limit, len(dq_units))
            while dq_pos[0] < limit:
                ua = dq_units[dq_pos[0]]
                ub = dq_units[dq_pos[0] + 1] if dq_pos[0] + 1 < limit else None
                emit_unit_pair(ua, ub)
                dq_pos[0] += 1 if ub is None else 2

        def ln_pass():
            for tt in range(NTT):
                kx = ("x", tt)
                pr.op("vector", lambda v, tt=tt: v.reduce_sum(
                    ln_mean[tt][:], x_sb[tt][:], axis=AX.X), reads=[kx], writes=[("lnm", tt)])
                pr.op("vector", lambda v, tt=tt: v.tensor_scalar(
                    ln_mean[tt][:], ln_mean[tt][:], 1.0 / D, None, OP.mult),
                    reads=[("lnm", tt)], writes=[("lnm", tt)])
                pr.op("scalar", lambda s, tt=tt: s.activation(
                    xn_sb[tt][:], x_sb[tt][:], AF.Square, bias=cc(C0), scale=cc(C1),
                    accum_out=ln_ssum[tt][:]),
                    reads=[kx, ("cst",)], writes=[("xn", tt), ("lns", tt)])
                pr.op("scalar", lambda s, tt=tt: s.activation(
                    ln_m2[tt][:], ln_mean[tt][:], AF.Square, bias=cc(C0), scale=cc(C1)),
                    reads=[("lnm", tt), ("cst",)], writes=[("ln2", tt)])
                pr.op("vector", lambda v, tt=tt: v.scalar_tensor_tensor(
                    ln_var[tt][:], ln_ssum[tt][:], 1.0 / D, ln_m2[tt][:], OP.mult, OP.subtract),
                    reads=[("lns", tt), ("ln2", tt)], writes=[("lnv", tt)])
                pr.op("scalar", lambda s, tt=tt: s.activation(
                    ln_var[tt][:], ln_var[tt][:], AF.Ln, bias=cc(EPSLN), scale=cc(C1)),
                    reads=[("lnv", tt), ("cst",)], writes=[("lnv", tt)])
                pr.op("scalar", lambda s, tt=tt: s.activation(
                    ln_rstd[tt][:], ln_var[tt][:], AF.Exp, bias=cc(C0), scale=cc(CM05)),
                    reads=[("lnv", tt), ("cst",)], writes=[("lnr", tt)])
                pr.op("vector", lambda v, tt=tt: v.tensor_scalar(
                    xn_sb[tt][:], x_sb[tt][:], ln_mean[tt][:], ln_rstd[tt][:],
                    OP.subtract, OP.mult),
                    reads=[kx, ("lnm", tt), ("lnr", tt)], writes=[("xn", tt)])

        def transpose_pass():
            for tt in range(NTT):
                for dt in range(NDT):
                    par = (tt * NDT + dt) % 2
                    ktp = ("pbtp", par)
                    pr.op("tensor", lambda t, tt=tt, dt=dt, par=par: t.transpose(
                        pb_tp[par].ap()[:, 0:P], xn_sb[tt].ap()[:, dt * P:(dt + 1) * P],
                        s_idm[:]),
                        reads=[("xn", tt), ("idm",), ktp], writes=[ktp], weak=True)
                    t_ev = pr.op("scalar", lambda s, tt=tt, dt=dt, par=par: s.activation(
                        xnT[dt].ap()[:, tt * P:(tt + 1) * P], pb_tp[par].ap()[:, 0:P], AF.Copy),
                        reads=[ktp], writes=[("xnT", dt)])
                    pr.cover([("xn", tt)], [t_ev])

        H5 = 512
        for ch in range(S // TCA):
            base = ch * TCA
            A = ATT[0]
            sx = 0
            for t8 in range(NTTA):
                xb = t8 % 2
                li = t8 % 4
                kx = ("x", xb)
                pr.dma("sync", lambda h, t8=t8, xb=xb, base=base: h.dma_start(
                    out=x_sb[xb][:], in_=x_in[base + t8 * P: base + (t8 + 1) * P, :]),
                    writes=[kx], sem=f"d_x{xb}")
                pr.op("vector", lambda v, xb=xb, li=li: v.reduce_sum(
                    ln_mean[li][:], x_sb[xb][:], axis=AX.X), reads=[kx], writes=[("lnm", li)])
                pr.op("vector", lambda v, li=li: v.tensor_scalar(
                    ln_mean[li][:], ln_mean[li][:], 1.0 / D, None, OP.mult),
                    reads=[("lnm", li)], writes=[("lnm", li)])
                pr.op("scalar", lambda s, xb=xb, li=li: s.activation(
                    xn_sb[xb][:], x_sb[xb][:], AF.Square, bias=cc(C0), scale=cc(C1),
                    accum_out=ln_ssum[li][:]),
                    reads=[kx, ("cst",)], writes=[("xn", xb), ("lns", li)])
                pr.op("scalar", lambda s, li=li: s.activation(
                    ln_m2[li][:], ln_mean[li][:], AF.Square, bias=cc(C0), scale=cc(C1)),
                    reads=[("lnm", li), ("cst",)], writes=[("ln2", li)])
                pr.op("vector", lambda v, li=li: v.scalar_tensor_tensor(
                    ln_var[li][:], ln_ssum[li][:], 1.0 / D, ln_m2[li][:], OP.mult, OP.subtract),
                    reads=[("lns", li), ("ln2", li)], writes=[("lnv", li)])
                pr.op("scalar", lambda s, li=li: s.activation(
                    ln_var[li][:], ln_var[li][:], AF.Ln, bias=cc(EPSLN), scale=cc(C1)),
                    reads=[("lnv", li), ("cst",)], writes=[("lnv", li)])
                pr.op("scalar", lambda s, li=li: s.activation(
                    ln_rstd[li][:], ln_var[li][:], AF.Exp, bias=cc(C0), scale=cc(CM05)),
                    reads=[("lnv", li), ("cst",)], writes=[("lnr", li)])
                pr.op("vector", lambda v, xb=xb, li=li: v.tensor_scalar(
                    xn_sb[xb][:], x_sb[xb][:], ln_mean[li][:], ln_rstd[li][:],
                    OP.subtract, OP.mult),
                    reads=[kx, ("lnm", li), ("lnr", li)], writes=[("xn", xb)])
                for dt in range(NDT):
                    par = dt % 2
                    ktp = ("pbtp", par)
                    pr.op("tensor", lambda t, xb=xb, dt=dt, par=par: t.transpose(
                        pb_tp[par].ap()[:, 0:P], xn_sb[xb].ap()[:, dt * P:(dt + 1) * P],
                        s_idm[:]),
                        reads=[("xn", xb), ("idm",), ktp], writes=[ktp], weak=True)
                    t_ev = pr.op("scalar", lambda s, t8=t8, dt=dt, par=par: s.activation(
                        xnT[dt].ap()[:, t8 * P:(t8 + 1) * P], pb_tp[par].ap()[:, 0:P], AF.Copy),
                        reads=[ktp], writes=[("xnT", dt)])
                    pr.cover([("xn", xb)], [t_ev])
                emit_dq(6)

            def emit_bcast_y(ht):
                for half in range(2):
                    hs = half * H5
                    pr.op("tensor", lambda t, hs=hs: t.matmul(
                        pb_misc[:], s_ones_r[:], A["row"].ap()[0:1, hs:hs + H5],
                        start=True, stop=True),
                        reads=[("rowb", sx), ("onesr",), ("pbm",)], writes=[("pbm",)], weak=True)
                    pr.op("vector", lambda v, hs=hs: v.tensor_tensor(
                        A["kc"].ap()[:, hs:hs + H5], A["kc"].ap()[:, hs:hs + H5],
                        pb_misc[:], OP.mult),
                        reads=[("akc", sx), ("pbm",)], writes=[("akc", sx)])
                    t_y = pr.op("vector", lambda v, ht=ht, hs=hs, half=half: v.tensor_tensor(
                        hy_sb[2 * ht + half][:], A["kc"].ap()[:, hs:hs + H5],
                        A["a1"].ap()[:, hs:hs + H5], OP.add),
                        reads=[("akc", sx), ("aa1", sx)], writes=[("hy", 2 * ht + half)])
                    pr.cover([("rowb", sx), ("onesr",)], [t_y])

            for ht in range(NH):
                for q in range(4):
                    slot = (ht * 4 + q) % NWIN
                    pr.dma("sync", lambda h, ht=ht, q=q, slot=slot: h.dma_start(
                        out=win_buf[slot].ap().rearrange("p (a c) -> p a c", a=4),
                        in_=win_bfh[ht, q * 4:(q + 1) * 4].transpose([1, 0, 2])),
                        reads=[("wst", ht // 4)], writes=[("winb", slot)], sem=f"d_win{slot}")
                ev_first = []
                for half in range(2):
                    hs = half * H5
                    for q in range(4):
                        slot = (ht * 4 + q) % NWIN
                        def g_emit(t, q=q, slot=slot, hs=hs):
                            last = None
                            for dto in range(4):
                                dt = q * 4 + dto
                                coff = dto * 5 * P
                                for pj in range(5):
                                    last = t.matmul(
                                        pb[pj][:],
                                        win_buf[slot].ap()[:, coff + pj * P: coff + (pj + 1) * P],
                                        xnT[dt].ap()[:, hs:hs + H5],
                                        start=(dt == 0), stop=(dt == NDT - 1))
                            return last
                        pr.op("tensor", g_emit,
                              reads=[("winb", slot)] + [("xnT", q * 4 + i) for i in range(4)] +
                                    [("pb", b) for b in range(5)],
                              writes=[("pb", b) for b in range(5)], weak=True)
                    if half == 0 and ht > 0:
                        emit_bcast_y(ht - 1)
                    t_k = pr.op("scalar", lambda s, hs=hs: s.activation(
                        A["kc"].ap()[:, hs:hs + H5], pb[1][:], AF.Tanh, bias=cc(C0), scale=cc(C1)),
                        reads=[("pb", 1), ("cst",)], writes=[("akc", sx)])
                    t_v = pr.op("vector", lambda v, hs=hs: v.tensor_copy(
                        A["vf"].ap()[:, hs:hs + H5], pb[2][:]),
                        reads=[("pb", 2)], writes=[("avf", sx)])
                    t_a = pr.op("scalar", lambda s, hs=hs: s.activation(
                        A["as_"].ap()[:, hs:hs + H5], pb[3][:], AF.Sigmoid, bias=cc(C2), scale=cc(C1)),
                        reads=[("pb", 3), ("cst",)], writes=[("aas", sx)])
                    pr.op("vector", lambda v, hs=hs: v.tensor_scalar(
                        A["as_"].ap()[:, hs:hs + H5], A["as_"].ap()[:, hs:hs + H5],
                        0.6, 0.9995, OP.max, OP.min),
                        reads=[("aas", sx)], writes=[("aas", sx)])
                    t_qf = pr.op("vector", lambda v, hs=hs: v.tensor_copy(
                        A["qf"].ap()[:, hs:hs + H5], pb[0][:]),
                        reads=[("pb", 0)], writes=[("aqf", sx)])
                    pr.op("vector", lambda v, hs=hs, half=half: v.tensor_tensor(
                        hy_sb[32 + half][:], A["qf"].ap()[:, hs:hs + H5],
                        A["qf"].ap()[:, hs:hs + H5], OP.mult),
                        reads=[("aqf", sx)], writes=[("hy", 32 + half)])
                    t_g = pr.op("scalar", lambda s, hs=hs: s.activation(
                        A["gs"].ap()[:, hs:hs + H5], pb[4][:], AF.Sigmoid, bias=cc(C0), scale=cc(C1)),
                        reads=[("pb", 4), ("cst",)], writes=[("ags", sx)])
                    if half == 1:
                        ev_first.append(t_k)
                    pr.op("tensor", lambda t, half=half: t.matmul(
                        pb_misc.ap()[0:1, :], s_ones_c[:], hy_sb[32 + half][:],
                        start=True, stop=True),
                        reads=[("hy", 32 + half), ("onesc",), ("pbm",)], writes=[("pbm",)], weak=True)
                    t_rs = pr.op("vector", lambda v, hs=hs: v.tensor_copy(
                        A["row"].ap()[0:1, hs:hs + H5], pb_misc.ap()[0:1, :]),
                        reads=[("pbm",)], writes=[("rowa", sx)])
                    pr.cover([("hy", 32 + half), ("onesc",)], [t_rs])
                pr.cover([("winb", (ht * 4 + q) % NWIN) for q in range(4)] +
                         [("xnT", d) for d in range(NDT)], ev_first[:1])
                pr.op("scalar", lambda s: s.activation(
                    A["row"][:], A["row"][:], AF.Ln, bias=cc(EPSQ, 1), scale=cc(I128, 1)),
                    reads=[("rowa", sx), ("cst",)], writes=[("rowa", sx)])
                pr.op("scalar", lambda s: s.activation(
                    A["row"][:], A["row"][:], AF.Exp, bias=cc(C0, 1), scale=cc(CM05, 1)),
                    reads=[("rowa", sx), ("cst",)], writes=[("rowb", sx)])
                pr.op("vector", lambda v: v.tensor_copy(
                    A["a0"][:], A["as_"][:]),
                    reads=[("aas", sx)], writes=[("aa0", sx)])
                pr.op("vector", lambda v: v.memset(A["a0"].ap()[:, 0:TCA:SCH], 0.0),
                    reads=[("aa0", sx)], writes=[("aa0", sx)])
                pr.op("vector", lambda v: v.tensor_tensor(
                    A["a1"][:], A["as_"][:], A["a0"][:], OP.subtract),
                    reads=[("aas", sx), ("aa0", sx)], writes=[("aa1", sx)])
                pr.op("vector", lambda v: v.tensor_tensor_scan(
                    A["pref"][:], A["a0"][:], A["a1"][:], 1.0, OP.mult, OP.add),
                    reads=[("aa0", sx), ("aa1", sx)], writes=[("apref", sx)])
                pr.op("vector", lambda v: v.tensor_scalar(
                    A["pref"][:], A["pref"][:], 1e6, 1.0, OP.mult, OP.min),
                    reads=[("apref", sx)], writes=[("apref", sx)])
                pr.op("vector", lambda v: v.tensor_tensor(
                    A["a1"][:], A["kc"][:], A["vf"][:], OP.mult),
                    reads=[("akc", sx), ("avf", sx)], writes=[("aa1", sx)])
                pr.op("vector", lambda v: v.tensor_tensor(
                    A["pref"][:], A["a1"][:], A["pref"][:], OP.mult),
                    reads=[("aa1", sx), ("apref", sx)], writes=[("apref", sx)])
                if ch == 0:
                    pr.op("vector", lambda v: v.tensor_tensor_scan(
                        A["a0"][:], A["as_"][:], A["pref"][:], 0.0, OP.mult, OP.add),
                        reads=[("aas", sx), ("apref", sx)], writes=[("aa0", sx)])
                else:
                    pr.op("vector", lambda v, ht=ht: v.tensor_tensor_scan(
                        A["a0"][:], A["as_"][:], A["pref"][:], s_state.ap()[:, ht:ht + 1],
                        OP.mult, OP.add),
                        reads=[("aas", sx), ("apref", sx), ("state", ht)], writes=[("aa0", sx)])
                if ch < S // TCA - 1:
                    pr.op("vector", lambda v, ht=ht: v.tensor_copy(
                        s_state.ap()[:, ht:ht + 1], A["a0"].ap()[:, TCA - 1:TCA]),
                        reads=[("aa0", sx)], writes=[("state", ht)])
                pr.op("vector", lambda v: v.tensor_tensor(
                    A["kc"][:], A["qf"][:], A["a0"][:], OP.mult),
                    reads=[("aqf", sx), ("aa0", sx)], writes=[("akc", sx)])
                pr.op("vector", lambda v: v.tensor_tensor(
                    A["kc"][:], A["kc"][:], A["gs"][:], OP.mult),
                    reads=[("akc", sx), ("ags", sx)], writes=[("akc", sx)])
                pr.op("vector", lambda v: v.tensor_tensor(
                    A["a1"][:], A["gs"][:], A["vf"][:], OP.mult),
                    reads=[("ags", sx), ("avf", sx)], writes=[("aa1", sx)])
                pr.op("vector", lambda v: v.tensor_tensor(
                    A["a1"][:], A["vf"][:], A["a1"][:], OP.subtract),
                    reads=[("avf", sx), ("aa1", sx)], writes=[("aa1", sx)])
                emit_dq(18)
            emit_bcast_y(NH - 1)

            for thalf in range(2):
                for oc in range(4):
                    for cg in range(4):
                        pr.dma("sync", lambda h, oc=oc, cg=cg: h.dma_start(
                            out=wsm_buf[cg].ap().rearrange("p (a c) -> p a c", a=4),
                            in_=wout_bfh[cg * 4:(cg + 1) * 4, :, oc * TC:(oc + 1) * TC].transpose([1, 0, 2])),
                            reads=[("wot", cg // 2)], writes=[("wsmr", 2 * cg), ("wsmr", 2 * cg + 1)],
                            sem=f"d_wsm{cg}")
                    def wo_emit(t, thalf=thalf, oc=oc):
                        last = None
                        for ct in range(NDT):
                            for tt in range(NTT):
                                tg = thalf * H5 + tt * P
                                last = t.matmul(
                                    pb[tt][:],
                                    hy_sb[2 * ct + tg // H5].ap()[:, (tg % H5):(tg % H5) + P],
                                    wsm_buf[ct // 4].ap()[:, (ct % 4) * TC:(ct % 4 + 1) * TC],
                                    start=(ct == 0), stop=(ct == NDT - 1))
                        return last
                    pr.op("tensor", wo_emit,
                          reads=[("hy", c2) for c2 in range(2 * NDT)] +
                                [("wsmr", r) for r in range(8)] +
                                [("pb", tt) for tt in range(NTT)],
                          writes=[("pb", tt) for tt in range(NTT)], weak=True)
                    ev = []
                    for tt in range(NTT):
                        sl = tt % 2
                        t_cp = pr.op("vector", lambda v, tt=tt, sl=sl: v.tensor_copy(
                            A["pref"].ap()[:, sl * H5: sl * H5 + H5], pb[tt][:]),
                            reads=[("pb", tt)], writes=[("apref", sl, "wo")])
                        ev.append(t_cp)
                        pr.dma("sync", lambda h, thalf=thalf, oc=oc, tt=tt, sl=sl, base=base: h.dma_start(
                            out=x1_hbm[base + thalf * H5 + tt * P: base + thalf * H5 + (tt + 1) * P,
                                       oc * TC:(oc + 1) * TC],
                            in_=A["pref"].ap()[:, sl * H5: sl * H5 + H5]),
                            reads=[("apref", sl, "wo")], writes=[("x1h", ch, thalf, oc, tt)],
                            sem="d_x1h")
                    pr.cover([("hy", c2) for c2 in range(2 * NDT)] +
                             [("wsmr", r) for r in range(8)], ev)
                    emit_dq(9)

        wgu_r = wgu_bf.rearrange("(dt p) (r io) -> dt p r io", p=P, io=IO)
        wdn_r = wdn_bf.rearrange("(it p) n -> it p n", p=P)
        for ch in range(NCH):
            base = ch * TC
            for tt in range(NTT):
                pr.dma("sync", lambda h, tt=tt, base=base: h.dma_start(
                    out=x_sb[tt][:], in_=x1_hbm[base + tt * P: base + (tt + 1) * P, :]),
                    writes=[("x", tt)],
                    extra=[pr.all_of("d_x1h"), pr.all_of("d_out")], sem=f"d_x{tt}")
                pr.dma("sync", lambda h, tt=tt, base=base: h.dma_start(
                    out=ATT[0]["a0"][:], in_=x_in[base + tt * P: base + (tt + 1) * P, 0:TCA]),
                    reads=[("aa0", 0)], writes=[("aa0", 0)], sem="d_xa0")
                pr.dma("sync", lambda h, tt=tt, base=base: h.dma_start(
                    out=ATT[0]["pref"][:], in_=x_in[base + tt * P: base + (tt + 1) * P, TCA:D]),
                    reads=[("apref", 0)], writes=[("apref", 0)], sem="d_xa1")
                pr.op("vector", lambda v, tt=tt: v.tensor_tensor(
                    x_sb[tt].ap()[:, 0:TCA], x_sb[tt].ap()[:, 0:TCA], ATT[0]["a0"][:], OP.add),
                    reads=[("x", tt), ("aa0", 0)], writes=[("x", tt)])
                pr.op("vector", lambda v, tt=tt: v.tensor_tensor(
                    x_sb[tt].ap()[:, TCA:D], x_sb[tt].ap()[:, TCA:D], ATT[0]["pref"][:], OP.add),
                    reads=[("x", tt), ("apref", 0)], writes=[("x", tt)])
            ln_pass()
            transpose_pass()
            emit_dq(20)

            for pair in range(NPAIR):
                emit_dq_upto(gup_need[min(pair // 2 + 1, NCCP - 1)])
                pbg = pb[0] if pair % 2 == 0 else pb[2]
                pbu = pb[1] if pair % 2 == 0 else pb[3]
                kg = ("pb", 0 if pair % 2 == 0 else 2)
                ku = ("pb", 1 if pair % 2 == 0 else 3)
                for q in range(4):
                    vs = (pair * 4 + q) % 8
                    tb, vo = vs % 4, (vs // 4) * 1024
                    extraw = [("winb", vs % 4)] if pair * 4 + q in (4, 5, 6, 7) else []
                    pr.dma("sync", lambda h, pair=pair, q=q, tb=tb, vo=vo: h.dma_start(
                        out=win_buf[tb].ap()[:, vo:vo + 512].rearrange("p (a c) -> p a c", a=4),
                        in_=wgu_r[q * 4:(q + 1) * 4, :, 0, pair * P:(pair + 1) * P].transpose(
                            [1, 0, 2])),
                        reads=[("wdeq", "d_wdeqg")], writes=[("winb", 8 + vs)] + extraw,
                        sem=f"d_win{tb}")
                    pr.dma("sync", lambda h, pair=pair, q=q, tb=tb, vo=vo: h.dma_start(
                        out=win_buf[tb].ap()[:, vo + 512:vo + 1024].rearrange("p (a c) -> p a c", a=4),
                        in_=wgu_r[q * 4:(q + 1) * 4, :, 1, pair * P:(pair + 1) * P].transpose(
                            [1, 0, 2])),
                        reads=[("wdeq", "d_wdeqg")], writes=[("winb", 8 + vs)],
                        sem=f"d_win{tb}")
                for q in range(4):
                    vs = (pair * 4 + q) % 8
                    tb, vo = vs % 4, (vs // 4) * 1024
                    def gu_mm(t, q=q, tb=tb, vo=vo, pbg=pbg, pbu=pbu):
                        last = None
                        for dto in range(4):
                            dt = q * 4 + dto
                            last = t.matmul(pbg[:],
                                            win_buf[tb].ap()[:, vo + dto * P:vo + (dto + 1) * P],
                                            xnT[dt].ap()[:, 0:TC], start=(dt == 0), stop=(dt == NDT - 1))
                            last = t.matmul(pbu[:],
                                            win_buf[tb].ap()[:, vo + 512 + dto * P: vo + 512 + (dto + 1) * P],
                                            xnT[dt].ap()[:, 0:TC], start=(dt == 0), stop=(dt == NDT - 1))
                        return last
                    pr.op("tensor", gu_mm,
                          reads=[("winb", 8 + vs)] + [("xnT", q * 4 + i) for i in range(4)] +
                                [kg, ku],
                          writes=[kg, ku], weak=True)
                t_si = pr.op("scalar", lambda s, pair=pair, pbg=pbg: s.activation(
                    hy_sb[pair][:], pbg[:], AF.Silu, bias=cc(C0), scale=cc(C1)),
                    reads=[kg, ("cst",)], writes=[("hy", pair)])
                t_h = pr.op("vector", lambda v, pair=pair, pbu=pbu: v.tensor_tensor(
                    hy_sb[pair][:], hy_sb[pair][:], pbu[:], OP.mult),
                    reads=[("hy", pair), ku], writes=[("hy", pair)])
                pr.cover([("winb", 8 + (pair * 4 + q) % 8) for q in range(4)] +
                         [("xnT", d) for d in range(NDT)], [t_si, t_h])
                if pair >= NPAIR - 12:
                    emit_dq(max(1, cdiv(len(dq_units) - n_gup_units, 10)))

            for hc in range(4):
                emit_dq_upto(dn_need[hc])
                since = set()
                for ip in range(NIT // 2):
                    reg = ip % 8
                    tb, half = reg % 4, reg // 4
                    pr.dma("sync", lambda h, hc=hc, ip=ip, tb=tb, half=half: h.dma_start(
                        out=wsm_buf[tb].ap()[:, half * 2 * TC: (half + 1) * 2 * TC].rearrange(
                            "p (a c) -> p a c", a=2),
                        in_=wdn_r[2 * ip:2 * ip + 2, :, hc * TC:(hc + 1) * TC].transpose([1, 0, 2])),
                        reads=[("wdeq", "d_wdeqd")], writes=[("wsmr", reg)], sem=f"d_wsm{tb}")
                    def dn_mm(t, ip=ip, tb=tb, half=half):
                        last = None
                        for k in range(2):
                            it = 2 * ip + k
                            for tt in range(NTT):
                                last = t.matmul(
                                    pb[tt][:],
                                    hy_sb[it].ap()[:, tt * P:(tt + 1) * P],
                                    wsm_buf[tb].ap()[:, (half * 2 + k) * TC:(half * 2 + k + 1) * TC],
                                    start=(it == 0), stop=(it == NIT - 1))
                        return last
                    pr.op("tensor", dn_mm,
                          reads=[("hy", 2 * ip), ("hy", 2 * ip + 1), ("wsmr", reg)] +
                                [("pb", tt) for tt in range(NTT)],
                          writes=[("pb", tt) for tt in range(NTT)], weak=True)
                    since.add(reg)
                    if (ip + 1) % 8 == 0 and ip != NIT // 2 - 1:
                        t_dr = pr.op("tensor", lambda t: t.drain(fusable=False))
                        pr.cover([("wsmr", r) for r in since], [t_dr])
                        since = set()
                ev = []
                for tt in range(NTT):
                    t_cp = pr.op("vector", lambda v, hc=hc, tt=tt: v.tensor_copy(
                        xn_sb[tt].ap()[:, hc * TC:(hc + 1) * TC], pb[tt][:]),
                        reads=[("pb", tt)], writes=[("xn", tt)])
                    t_sq = pr.op("scalar", lambda s, hc=hc, tt=tt: s.activation(
                        ATT[0]["a0"].ap()[:, 0:TC], pb[tt][:], AF.Square, bias=cc(C0), scale=cc(C1),
                        accum_out=f_ssum.ap()[:, tt * 4 + hc: tt * 4 + hc + 1]),
                        reads=[("pb", tt), ("cst",)],
                        writes=[("aa0", 0), ("fss", tt, hc)])
                    ev += [t_cp, t_sq]
                pr.cover([("hy", it) for it in range(NIT)] +
                         [("wsmr", r) for r in range(8)], ev)
            for tt in range(NTT):
                pr.op("vector", lambda v, tt=tt: v.reduce_sum(
                    f_tot[tt][:], f_ssum.ap()[:, tt * 4:(tt + 1) * 4], axis=AX.X),
                    reads=[("fss", tt, h2) for h2 in range(4)], writes=[("ftot", tt)])
                pr.op("scalar", lambda s, tt=tt: s.activation(
                    f_tot[tt][:], f_tot[tt][:], AF.Ln, bias=cc(EPSF), scale=cc(I2048)),
                    reads=[("ftot", tt), ("cst",)], writes=[("ftot", tt)])
                pr.op("scalar", lambda s, tt=tt: s.activation(
                    f_tot[tt][:], f_tot[tt][:], AF.Exp, bias=cc(C0), scale=cc(CM05)),
                    reads=[("ftot", tt), ("cst",)], writes=[("ftot", tt)])
                pr.op("vector", lambda v, tt=tt: v.scalar_tensor_tensor(
                    x_sb[tt][:], xn_sb[tt][:], f_tot[tt][:], x_sb[tt][:], OP.mult, OP.add),
                    reads=[("xn", tt), ("ftot", tt), ("x", tt)], writes=[("x", tt)])
                pr.dma("sync", lambda h, tt=tt, base=base: h.dma_start(
                    out=out_hbm[base + tt * P: base + (tt + 1) * P, :], in_=x_sb[tt][:]),
                    reads=[("x", tt)], writes=[("outh", ch, tt)], sem="d_out")

        pr.op("sync", lambda h: h.nop(), extra=[pr.all_of("d_out")])
        pr.emit(block, sems)
    return nc


def _host_layouts(w_in, w_out, gup_packed, gup_scales, dn_packed, dn_scales):
    wT = np.ascontiguousarray(w_in.T)
    win_blk = np.ascontiguousarray(
        wT.reshape(NDT, P, 5, NH, P).transpose(3, 0, 1, 2, 4).reshape(NH, NDT, P, 5 * P))
    wout_blk = np.ascontiguousarray(w_out.T.reshape(NDT, P, D))
    gup_pk = np.ascontiguousarray(gup_packed.T)
    gup_sc = np.ascontiguousarray(np.repeat(gup_scales, 16, axis=0))
    dn_pk = np.ascontiguousarray(dn_packed.T)
    dn_sc = np.ascontiguousarray(np.repeat(dn_scales, 16, axis=0))
    idm = np.eye(P, dtype=np.float32).astype(ml_dtypes.bfloat16)
    cst = np.broadcast_to(np.array(CONST_VALS + [0.0, 0.0], np.float32), (P, 12)).copy()
    return win_blk, wout_blk, gup_pk, gup_sc, dn_pk, dn_sc, idm, cst


def kernel(L0_attn_x, L0_attn_w_in, L0_attn_w_out, L0_ffn_gate_up_packed,
           L0_ffn_gate_up_scales, L0_ffn_down_packed, L0_ffn_down_scales):
    x = np.ascontiguousarray(np.asarray(L0_attn_x, dtype=np.float32))
    B, S, Dm = x.shape
    assert Dm == D
    win_blk, wout_blk, gup_pk, gup_sc, dn_pk, dn_sc, idm, cst = _host_layouts(
        np.asarray(L0_attn_w_in, np.float32), np.asarray(L0_attn_w_out, np.float32),
        np.asarray(L0_ffn_gate_up_packed, np.int32),
        np.asarray(L0_ffn_gate_up_scales, np.float16),
        np.asarray(L0_ffn_down_packed, np.int32),
        np.asarray(L0_ffn_down_scales, np.float16))
    nc = build_nc(S)
    shared = dict(win_blk=win_blk, wout_blk=wout_blk, gup_pk=gup_pk, gup_sc=gup_sc,
                  dn_pk=dn_pk, dn_sc=dn_sc, idm=idm, cst=cst)
    in_maps = [dict(x_in=np.ascontiguousarray(x[c]), **shared) for c in range(B)]
    res = run_bass_kernel_spmd(nc, in_maps, core_ids=list(range(B)))
    out = np.stack([np.asarray(res.results[c]["out"], dtype=np.float32) for c in range(B)])
    return out


# version 15 (final)
# speedup vs baseline: 1.3317x; 1.3252x over previous
import numpy as np
import ml_dtypes
import concourse.bass as bass
import concourse.mybir as mybir
from concourse.bass_utils import run_bass_kernel_spmd
from contextlib import ExitStack

F32 = mybir.dt.float32
F16 = mybir.dt.float16
BF16 = mybir.dt.bfloat16
I32 = mybir.dt.int32
AF = mybir.ActivationFunctionType
OP = mybir.AluOpType
AX = mybir.AxisListType

P = 128
D = 2048
NDT = D // P
NH = 16
IO = 5632
NIT = IO // P
NPAIR = IO // P
SCH = 64
TC = 512
NTT = TC // P
GUW = D // 8
DNW = IO // 8
DQW = 256
TCA = 1024
NTTA = TCA // P
NWIN = 4

ENGS = ("sync", "scalar", "vector", "gpsimd", "tensor")


class Tok:
    __slots__ = ("sem", "val", "weak")
    def __init__(self, sem, val, weak=False):
        self.sem = sem
        self.val = val
        self.weak = weak


class Prog:
    def __init__(self):
        self.ops = {e: [] for e in ENGS}
        self.count = {e: 0 for e in ENGS}
        self.buf = {}

    def op(self, eng, emit, reads=(), writes=(), extra=(), inc=1, sem=None, weak=False):
        reads = list(reads)
        writes = list(writes)
        for k in list(reads):
            if isinstance(k, tuple) and k and str(k[0]).startswith("pb"):
                reads.remove(k)
                writes.append(k)
        deps = list(extra)
        for k in reads:
            st = self.buf.get(k)
            if st and st[0] is not None:
                deps.append(st[0])
        for k in writes:
            st = self.buf.get(k)
            if st:
                if st[0] is not None:
                    deps.append(st[0])
                for r in st[1]:
                    assert not r.weak, f"WAR on weak (uncovered) PE reader of {k}"
                    deps.append(r)
        sem = sem or eng
        self.count[sem] = self.count.get(sem, 0) + inc
        tok = Tok(sem, self.count[sem], weak)
        self.ops[eng].append((deps, emit, inc, tok))
        for k in reads:
            self.buf.setdefault(k, [None, []])[1].append(tok)
        for k in writes:
            self.buf[k] = [tok, []]
        return tok

    def dma(self, eng, emit, reads=(), writes=(), extra=(), sem=None):
        assert sem is not None
        return self.op(eng, emit, reads, writes, extra, inc=16, sem=sem)

    def cover(self, keys, toks):
        for k in keys:
            st = self.buf.get(k)
            if not st:
                continue
            st[1] = [r for r in st[1] if not r.weak] + list(toks)

    def all_of(self, sem):
        return Tok(sem, self.count.get(sem, 0))

    def emit(self, block, sems):
        for eng in ENGS:
            ops = self.ops[eng]
            if not ops:
                continue

            def make_section(eng, ops):
                def section(h):
                    seen = {}
                    for deps, emit, inc, tok in ops:
                        need = {}
                        for d in deps:
                            if d.val > need.get(d.sem, 0):
                                need[d.sem] = d.val
                        for dsem in sorted(need):
                            dval = need[dsem]
                            if dsem == tok.sem:
                                dval = min(dval, tok.val - inc)
                            if dval <= seen.get(dsem, 0):
                                continue
                            h.wait_ge(sems[dsem], dval)
                            seen[dsem] = dval
                        emit(h).then_inc(sems[tok.sem], inc)
                return section
            getattr(block, eng)(make_section(eng, ops))


def cdiv(a, b):
    return (a + b - 1) // b


C2, C1, CM1, EPSLN, CM05, EPSQ, I128, C0, EPSF, I2048 = range(10)
CONST_VALS = [2.0, 1.0, -1.0, 1e-5, -0.5, 1.1920928955078125e-07,
              1.0 / 128.0, 0.0, 1e-6, 1.0 / 2048.0]


def build_nc(S):
    NCH = S // TC
    nc = bass.Bass("TRN2", target_bir_lowering=False)

    x_in = nc.dram_tensor("x_in", [S, D], F32, kind="ExternalInput").ap()
    win_blk = nc.dram_tensor("win_blk", [NH, NDT, P, 5 * P], F32, kind="ExternalInput").ap()
    wout_blk = nc.dram_tensor("wout_blk", [NDT, P, D], F32, kind="ExternalInput").ap()
    gup_pk = nc.dram_tensor("gup_pk", [GUW, 2 * IO], I32, kind="ExternalInput").ap()
    gup_sc = nc.dram_tensor("gup_sc", [GUW, 2 * IO], F16, kind="ExternalInput").ap()
    dn_pk = nc.dram_tensor("dn_pk", [DNW, D], I32, kind="ExternalInput").ap()
    dn_sc = nc.dram_tensor("dn_sc", [DNW, D], F16, kind="ExternalInput").ap()
    idm_in = nc.dram_tensor("idm", [P, P], BF16, kind="ExternalInput").ap()
    cst_in = nc.dram_tensor("cst", [P, 12], F32, kind="ExternalInput").ap()
    out_hbm = nc.dram_tensor("out", [S, D], F32, kind="ExternalOutput").ap()

    wgu_bf = nc.dram_tensor("wgu_bf", [D, 2 * IO], BF16, kind="Internal").ap()
    wdn_bf = nc.dram_tensor("wdn_bf", [IO, D], BF16, kind="Internal").ap()
    x1_hbm = nc.dram_tensor("x1_hbm", [S, D], F32, kind="Internal").ap()
    win_bfh = nc.dram_tensor("win_bfh", [NH, NDT, P, 5 * P], BF16, kind="Internal").ap()
    wout_bfh = nc.dram_tensor("wout_bfh", [NDT, P, D], BF16, kind="Internal").ap()

    es = ExitStack()
    sb = lambda n, sh, dt: es.enter_context(nc.sbuf_tensor(n, sh, dt))
    ps = lambda n, sh, dt: es.enter_context(nc.psum_tensor(n, sh, dt))

    s_cst = sb("s_cst", [P, 12], F32)
    s_idm = sb("s_idm", [P, P], BF16)
    x_sb = [sb(f"x{t}", [P, D], F32) for t in range(NTT)]
    xn_sb = [sb(f"xn{t}", [P, D], BF16) for t in range(NTT)]
    xnT = [sb(f"xnT{d}", [P, TCA], BF16) for d in range(NDT)]
    win_buf = [sb(f"winb{i}", [P, 4 * 5 * P], BF16) for i in range(NWIN)]
    wsm_buf = [sb(f"wsmb{i}", [P, 4 * TC], BF16) for i in range(4)]
    hy_sb = [sb(f"hy{i}", [P, TC], BF16) for i in range(NIT)]

    def attn_set(s):
        return dict(
            qf=sb(f"a_qf{s}", [P, TCA], F32), vf=sb(f"a_vf{s}", [P, TCA], F32),
            kc=sb(f"a_kc{s}", [P, TCA], F32), as_=sb(f"a_as{s}", [P, TCA], F32),
            gs=sb(f"a_gs{s}", [P, TCA], F32),
            a0=sb(f"a_a0{s}", [P, TCA], F32), a1=sb(f"a_a1{s}", [P, TCA], F32),
            pref=sb(f"a_pref{s}", [P, TCA], F32),
            row=sb(f"s_row{s}", [1, TCA], F32))
    ATT = [attn_set(0)]
    s_ones_c = sb("s_ones_c", [P, 1], BF16)
    s_ones_r = sb("s_ones_r", [1, P], F32)
    s_state = sb("s_state", [P, NH], F32)
    ln_mean = [sb(f"lnm{t}", [P, 1], F32) for t in range(NTT)]
    ln_ssum = [sb(f"lns{t}", [P, 1], F32) for t in range(NTT)]
    ln_m2 = [sb(f"ln2_{t}", [P, 1], F32) for t in range(NTT)]
    ln_var = [sb(f"lnv{t}", [P, 1], F32) for t in range(NTT)]
    ln_rstd = [sb(f"lnr{t}", [P, 1], F32) for t in range(NTT)]
    f_ssum = sb("f_ssum", [P, NTT * 4], F32)
    f_tot = [sb(f"ftot{t}", [P, 1], F32) for t in range(NTT)]
    dq_pk = [sb(f"dq_pk{i}", [P, DQW], I32) for i in range(2)]
    dq_sc = [sb(f"dq_sc{i}", [P, DQW], F16) for i in range(2)]
    dq_em = [sb(f"dq_em{i}", [P, DQW], I32) for i in range(2)]
    dq_sg = [sb(f"dq_sg{i}", [P, DQW], I32) for i in range(2)]
    dq_r1 = [sb(f"dq_r1{i}", [P, DQW], F32) for i in range(2)]
    dq_o = [sb(f"dq_o{i}", [P, DQW], BF16) for i in range(2)]

    pb = [ps(f"pb{i}", [P, TC], F32) for i in range(5)]
    pb_misc = ps("pb_misc", [P, TC], F32)
    pb_tp = [ps(f"pbtp{i}", [P, 1024], BF16) for i in range(2)]

    DMA_SEMS = (["d_cst", "d_idm", "d_dqpk0", "d_dqpk1", "d_dqsc0", "d_dqsc1",
                 "d_wdeqg", "d_wdeqd", "d_x1h", "d_out"] +
                [f"d_x{i}" for i in range(NTT)] +
                [f"d_win{i}" for i in range(NWIN)] +
                [f"d_wsm{i}" for i in range(8)] + ["d_xa0", "d_xa1"] +
                [f"d_wst{i}" for i in range(4)] + [f"d_wot{i}" for i in range(2)])
    sems = {e: es.enter_context(nc.semaphore(f"sem_{e}")) for e in ENGS}
    for snm in DMA_SEMS:
        sems[snm] = es.enter_context(nc.semaphore(snm))
    block = es.enter_context(nc.Block())

    pr = Prog()

    def cc(i, wp=P):
        return s_cst.ap()[0:wp, i:i + 1]

    with es:
        for g in range(4):
            pr.dma("gpsimd", lambda h, g=g: h.dma_start(
                out=win_bfh[g * 4:(g + 1) * 4].rearrange("a b p c -> (a b p) c"),
                in_=win_blk[g * 4:(g + 1) * 4].rearrange("a b p c -> (a b p) c")),
                writes=[("wst", g)], sem=f"d_wst{g}")
        for g in range(2):
            pr.dma("gpsimd", lambda h, g=g: h.dma_start(
                out=wout_bfh[g * 8:(g + 1) * 8].rearrange("a p c -> (a p) c"),
                in_=wout_blk[g * 8:(g + 1) * 8].rearrange("a p c -> (a p) c")),
                writes=[("wot", g)], sem=f"d_wot{g}")
        pr.dma("sync", lambda h: h.dma_start(out=s_cst[:], in_=cst_in),
               writes=[("cst",)], sem="d_cst")
        pr.dma("sync", lambda h: h.dma_start(out=s_idm[:], in_=idm_in),
               writes=[("idm",)], sem="d_idm")
        pr.op("vector", lambda v: v.memset(s_ones_c[:], 1.0), writes=[("onesc",)])
        pr.op("vector", lambda v: v.memset(s_ones_r[:], 1.0), writes=[("onesr",)])
        pr.op("vector", lambda v: v.memset(s_state[:], 0.0),
              writes=[("state", i) for i in range(NH)])

        dq_par = [0, 0]

        def dq_stage_ops(u):
            pk_hbm, sc_hbm, dst, w0, wp, c0, cw, j, pa, ga = u
            kpk, ksc = ("dqpk", ga), ("dqsc", ga)
            kem, ksg, kr1, ko = ("dqem", pa), ("dqsg", pa), ("dqr1", pa), ("dqo", pa)
            em, sg, r1, o, pk, sc = (dq_em[pa], dq_sg[pa], dq_r1[pa], dq_o[pa],
                                     dq_pk[ga], dq_sc[ga])
            wsem = "d_wdeqg" if dst is wgu_bf else "d_wdeqd"

            def s_load():
                if j == 0:
                    pr.dma("sync", lambda h: h.dma_start(
                        out=pk[0:wp, 0:cw], in_=pk_hbm[w0:w0 + wp, c0:c0 + cw]),
                        writes=[kpk], sem=f"d_dqpk{ga}")
                    pr.dma("sync", lambda h: h.dma_start(
                        out=sc[0:wp, 0:cw], in_=sc_hbm[w0:w0 + wp, c0:c0 + cw]),
                        writes=[ksc], sem=f"d_dqsc{ga}")

            def s_trio():
                sh = 22 - 4 * j
                if sh >= 0:
                    pr.op("vector", lambda v: v.tensor_scalar(
                        em[0:wp, 0:cw], pk[0:wp, 0:cw], sh, 7 << 22,
                        OP.logical_shift_left, OP.bitwise_and), reads=[kpk], writes=[kem])
                else:
                    pr.op("vector", lambda v: v.tensor_scalar(
                        em[0:wp, 0:cw], pk[0:wp, 0:cw], -sh, 7 << 22,
                        OP.logical_shift_right, OP.bitwise_and), reads=[kpk], writes=[kem])
                pr.op("vector", lambda v: v.tensor_scalar(
                    sg[0:wp, 0:cw], pk[0:wp, 0:cw], 28 - 4 * j, -2147483648,
                    OP.logical_shift_left, OP.bitwise_and), reads=[kpk], writes=[ksg])
                pr.op("vector", lambda v: v.tensor_scalar(
                    em[0:wp, 0:cw], em[0:wp, 0:cw], 252 << 22, None, OP.add),
                    reads=[kem], writes=[kem])

            def s_relu():
                pr.op("scalar", lambda s: s.activation(
                    r1[0:wp, 0:cw], em.ap()[0:wp, 0:cw].bitcast(F32), AF.Relu,
                    bias=cc(C1, wp), scale=cc(CM1, wp)),
                    reads=[kem, ("cst",)], writes=[kr1])

            def s_vp():
                pr.op("gpsimd", lambda g: g.tensor_tensor(
                    r1[0:wp, 0:cw], em.ap()[0:wp, 0:cw].bitcast(F32), r1[0:wp, 0:cw],
                    OP.subtract), reads=[kem, kr1], writes=[kr1])

            def s_or():
                pr.op("vector", lambda v: v.tensor_tensor(
                    r1.ap()[0:wp, 0:cw].bitcast(I32), r1.ap()[0:wp, 0:cw].bitcast(I32),
                    sg[0:wp, 0:cw], OP.bitwise_or),
                    reads=[kr1, ksg], writes=[kr1])

            def s_sc():
                pr.op("gpsimd", lambda g: g.tensor_tensor(
                    o[0:wp, 0:cw], r1[0:wp, 0:cw], sc[0:wp, 0:cw], OP.mult),
                    reads=[kr1, ksc], writes=[ko])
                pr.dma("sync", lambda h: h.dma_start(
                    out=dst.rearrange("(w e) n -> w e n", e=8)[w0:w0 + wp, j, c0:c0 + cw],
                    in_=o[0:wp, 0:cw]),
                    reads=[ko], writes=[("wdeq", wsem)], sem=wsem)
            return [s_load, s_trio, s_relu, s_vp, s_or, s_sc]

        def emit_unit_pair(ua, ub):
            sa = dq_stage_ops(ua)
            sb = dq_stage_ops(ub) if ub is not None else None
            for i in range(6):
                sa[i]()
                if sb is not None:
                    sb[i]()

        dq_units = []
        NCCP = IO // DQW
        pa_ctr = [0, 0]

        def add_unit(pk_hbm, sc_hbm, dst, w0, wp, c0, cw, j):
            pa = pa_ctr[0] & 1
            pa_ctr[0] += 1
            ga = pa_ctr[1] & 1
            if j == 7:
                pa_ctr[1] += 1
            dq_units.append((pk_hbm, sc_hbm, dst, w0, wp, c0, cw, j, pa, ga))

        gup_need = []
        for g in range(NCCP):
            for ci in (g, NCCP + g):
                c0, cw = ci * DQW, DQW
                for wt in range(cdiv(GUW, P)):
                    w0, wp = wt * P, min(P, GUW - wt * P)
                    for j in range(8):
                        add_unit(gup_pk, gup_sc, wgu_bf, w0, wp, c0, cw, j)
            gup_need.append(len(dq_units))
        n_gup_units = len(dq_units)
        dn_need = []
        for hc in range(4):
            for ci in (2 * hc, 2 * hc + 1):
                c0, cw = ci * DQW, DQW
                for wt in range(cdiv(DNW, P)):
                    w0, wp = wt * P, min(P, DNW - wt * P)
                    for j in range(8):
                        add_unit(dn_pk, dn_sc, wdn_bf, w0, wp, c0, cw, j)
            dn_need.append(len(dq_units))
        dq_pos = [0]

        def emit_dq(n):
            stop = min(len(dq_units), dq_pos[0] + n)
            while dq_pos[0] < stop:
                ua = dq_units[dq_pos[0]]
                ub = dq_units[dq_pos[0] + 1] if dq_pos[0] + 1 < stop else None
                emit_unit_pair(ua, ub)
                dq_pos[0] += 1 if ub is None else 2

        def emit_dq_upto(limit):
            limit = min(limit, len(dq_units))
            while dq_pos[0] < limit:
                ua = dq_units[dq_pos[0]]
                ub = dq_units[dq_pos[0] + 1] if dq_pos[0] + 1 < limit else None
                emit_unit_pair(ua, ub)
                dq_pos[0] += 1 if ub is None else 2

        def ln_pass():
            for tt in range(NTT):
                kx = ("x", tt)
                pr.op("vector", lambda v, tt=tt: v.reduce_sum(
                    ln_mean[tt][:], x_sb[tt][:], axis=AX.X), reads=[kx], writes=[("lnm", tt)])
                pr.op("vector", lambda v, tt=tt: v.tensor_scalar(
                    ln_mean[tt][:], ln_mean[tt][:], 1.0 / D, None, OP.mult),
                    reads=[("lnm", tt)], writes=[("lnm", tt)])
                pr.op("scalar", lambda s, tt=tt: s.activation(
                    xn_sb[tt][:], x_sb[tt][:], AF.Square, bias=cc(C0), scale=cc(C1),
                    accum_out=ln_ssum[tt][:]),
                    reads=[kx, ("cst",)], writes=[("xn", tt), ("lns", tt)])
                pr.op("scalar", lambda s, tt=tt: s.activation(
                    ln_m2[tt][:], ln_mean[tt][:], AF.Square, bias=cc(C0), scale=cc(C1)),
                    reads=[("lnm", tt), ("cst",)], writes=[("ln2", tt)])
                pr.op("vector", lambda v, tt=tt: v.scalar_tensor_tensor(
                    ln_var[tt][:], ln_ssum[tt][:], 1.0 / D, ln_m2[tt][:], OP.mult, OP.subtract),
                    reads=[("lns", tt), ("ln2", tt)], writes=[("lnv", tt)])
                pr.op("scalar", lambda s, tt=tt: s.activation(
                    ln_var[tt][:], ln_var[tt][:], AF.Ln, bias=cc(EPSLN), scale=cc(C1)),
                    reads=[("lnv", tt), ("cst",)], writes=[("lnv", tt)])
                pr.op("scalar", lambda s, tt=tt: s.activation(
                    ln_rstd[tt][:], ln_var[tt][:], AF.Exp, bias=cc(C0), scale=cc(CM05)),
                    reads=[("lnv", tt), ("cst",)], writes=[("lnr", tt)])
                pr.op("vector", lambda v, tt=tt: v.tensor_scalar(
                    xn_sb[tt][:], x_sb[tt][:], ln_mean[tt][:], ln_rstd[tt][:],
                    OP.subtract, OP.mult),
                    reads=[kx, ("lnm", tt), ("lnr", tt)], writes=[("xn", tt)])

        def transpose_pass():
            for tt in range(NTT):
                for dt in range(NDT):
                    par = (tt * NDT + dt) % 2
                    ktp = ("pbtp", par)
                    pr.op("tensor", lambda t, tt=tt, dt=dt, par=par: t.transpose(
                        pb_tp[par].ap()[:, 0:P], xn_sb[tt].ap()[:, dt * P:(dt + 1) * P],
                        s_idm[:]),
                        reads=[("xn", tt), ("idm",), ktp], writes=[ktp], weak=True)
                    t_ev = pr.op("scalar", lambda s, tt=tt, dt=dt, par=par: s.activation(
                        xnT[dt].ap()[:, tt * P:(tt + 1) * P], pb_tp[par].ap()[:, 0:P], AF.Copy),
                        reads=[ktp], writes=[("xnT", dt)])
                    pr.cover([("xn", tt)], [t_ev])

        H5 = 512
        for ch in range(S // TCA):
            base = ch * TCA
            A = ATT[0]
            sx = 0
            for t8 in range(NTTA):
                xb = t8 % 2
                li = t8 % 4
                kx = ("x", xb)
                pr.dma("sync", lambda h, t8=t8, xb=xb, base=base: h.dma_start(
                    out=x_sb[xb][:], in_=x_in[base + t8 * P: base + (t8 + 1) * P, :]),
                    writes=[kx], sem=f"d_x{xb}")
                pr.op("vector", lambda v, xb=xb, li=li: v.reduce_sum(
                    ln_mean[li][:], x_sb[xb][:], axis=AX.X), reads=[kx], writes=[("lnm", li)])
                pr.op("vector", lambda v, li=li: v.tensor_scalar(
                    ln_mean[li][:], ln_mean[li][:], 1.0 / D, None, OP.mult),
                    reads=[("lnm", li)], writes=[("lnm", li)])
                pr.op("scalar", lambda s, xb=xb, li=li: s.activation(
                    xn_sb[xb][:], x_sb[xb][:], AF.Square, bias=cc(C0), scale=cc(C1),
                    accum_out=ln_ssum[li][:]),
                    reads=[kx, ("cst",)], writes=[("xn", xb), ("lns", li)])
                pr.op("scalar", lambda s, li=li: s.activation(
                    ln_m2[li][:], ln_mean[li][:], AF.Square, bias=cc(C0), scale=cc(C1)),
                    reads=[("lnm", li), ("cst",)], writes=[("ln2", li)])
                pr.op("vector", lambda v, li=li: v.scalar_tensor_tensor(
                    ln_var[li][:], ln_ssum[li][:], 1.0 / D, ln_m2[li][:], OP.mult, OP.subtract),
                    reads=[("lns", li), ("ln2", li)], writes=[("lnv", li)])
                pr.op("scalar", lambda s, li=li: s.activation(
                    ln_var[li][:], ln_var[li][:], AF.Ln, bias=cc(EPSLN), scale=cc(C1)),
                    reads=[("lnv", li), ("cst",)], writes=[("lnv", li)])
                pr.op("scalar", lambda s, li=li: s.activation(
                    ln_rstd[li][:], ln_var[li][:], AF.Exp, bias=cc(C0), scale=cc(CM05)),
                    reads=[("lnv", li), ("cst",)], writes=[("lnr", li)])
                pr.op("vector", lambda v, xb=xb, li=li: v.tensor_scalar(
                    xn_sb[xb][:], x_sb[xb][:], ln_mean[li][:], ln_rstd[li][:],
                    OP.subtract, OP.mult),
                    reads=[kx, ("lnm", li), ("lnr", li)], writes=[("xn", xb)])
                for dt in range(NDT):
                    par = dt % 2
                    ktp = ("pbtp", par)
                    pr.op("tensor", lambda t, xb=xb, dt=dt, par=par: t.transpose(
                        pb_tp[par].ap()[:, 0:P], xn_sb[xb].ap()[:, dt * P:(dt + 1) * P],
                        s_idm[:]),
                        reads=[("xn", xb), ("idm",), ktp], writes=[ktp], weak=True)
                    t_ev = pr.op("scalar", lambda s, t8=t8, dt=dt, par=par: s.activation(
                        xnT[dt].ap()[:, t8 * P:(t8 + 1) * P], pb_tp[par].ap()[:, 0:P], AF.Copy),
                        reads=[ktp], writes=[("xnT", dt)])
                    pr.cover([("xn", xb)], [t_ev])
                emit_dq(6)

            def emit_bcast_y(ht):
                for half in range(2):
                    hs = half * H5
                    pr.op("tensor", lambda t, hs=hs: t.matmul(
                        pb_misc[:], s_ones_r[:], A["row"].ap()[0:1, hs:hs + H5],
                        start=True, stop=True),
                        reads=[("rowb", sx), ("onesr",), ("pbm",)], writes=[("pbm",)], weak=True)
                    pr.op("vector", lambda v, hs=hs: v.tensor_tensor(
                        A["kc"].ap()[:, hs:hs + H5], A["kc"].ap()[:, hs:hs + H5],
                        pb_misc[:], OP.mult),
                        reads=[("akc", sx), ("pbm",)], writes=[("akc", sx)])
                    t_y = pr.op("vector", lambda v, ht=ht, hs=hs, half=half: v.tensor_tensor(
                        hy_sb[2 * ht + half][:], A["kc"].ap()[:, hs:hs + H5],
                        A["a1"].ap()[:, hs:hs + H5], OP.add),
                        reads=[("akc", sx), ("aa1", sx)], writes=[("hy", 2 * ht + half)])
                    pr.cover([("rowb", sx), ("onesr",)], [t_y])

            for ht in range(NH):
                for q in range(4):
                    slot = (ht * 4 + q) % NWIN
                    pr.dma("sync", lambda h, ht=ht, q=q, slot=slot: h.dma_start(
                        out=win_buf[slot].ap().rearrange("p (a c) -> p a c", a=4),
                        in_=win_bfh[ht, q * 4:(q + 1) * 4].transpose([1, 0, 2])),
                        reads=[("wst", ht // 4)], writes=[("winb", slot)], sem=f"d_win{slot}")
                ev_first = []
                for half in range(2):
                    hs = half * H5
                    for q in range(4):
                        slot = (ht * 4 + q) % NWIN
                        def g_emit(t, q=q, slot=slot, hs=hs):
                            last = None
                            for dto in range(4):
                                dt = q * 4 + dto
                                coff = dto * 5 * P
                                for pj in range(5):
                                    last = t.matmul(
                                        pb[pj][:],
                                        win_buf[slot].ap()[:, coff + pj * P: coff + (pj + 1) * P],
                                        xnT[dt].ap()[:, hs:hs + H5],
                                        start=(dt == 0), stop=(dt == NDT - 1))
                            return last
                        pr.op("tensor", g_emit,
                              reads=[("winb", slot)] + [("xnT", q * 4 + i) for i in range(4)] +
                                    [("pb", b) for b in range(5)],
                              writes=[("pb", b) for b in range(5)], weak=True)
                    if half == 0 and ht > 0:
                        emit_bcast_y(ht - 1)
                    t_k = pr.op("scalar", lambda s, hs=hs: s.activation(
                        A["kc"].ap()[:, hs:hs + H5], pb[1][:], AF.Tanh, bias=cc(C0), scale=cc(C1)),
                        reads=[("pb", 1), ("cst",)], writes=[("akc", sx)])
                    t_v = pr.op("vector", lambda v, hs=hs: v.tensor_copy(
                        A["vf"].ap()[:, hs:hs + H5], pb[2][:]),
                        reads=[("pb", 2)], writes=[("avf", sx)])
                    t_a = pr.op("scalar", lambda s, hs=hs: s.activation(
                        A["as_"].ap()[:, hs:hs + H5], pb[3][:], AF.Sigmoid, bias=cc(C2), scale=cc(C1)),
                        reads=[("pb", 3), ("cst",)], writes=[("aas", sx)])
                    pr.op("vector", lambda v, hs=hs: v.tensor_scalar(
                        A["as_"].ap()[:, hs:hs + H5], A["as_"].ap()[:, hs:hs + H5],
                        0.6, 0.9995, OP.max, OP.min),
                        reads=[("aas", sx)], writes=[("aas", sx)])
                    t_qf = pr.op("vector", lambda v, hs=hs: v.tensor_copy(
                        A["qf"].ap()[:, hs:hs + H5], pb[0][:]),
                        reads=[("pb", 0)], writes=[("aqf", sx)])
                    pr.op("vector", lambda v, hs=hs, half=half: v.tensor_tensor(
                        hy_sb[32 + half][:], A["qf"].ap()[:, hs:hs + H5],
                        A["qf"].ap()[:, hs:hs + H5], OP.mult),
                        reads=[("aqf", sx)], writes=[("hy", 32 + half)])
                    t_g = pr.op("scalar", lambda s, hs=hs: s.activation(
                        A["gs"].ap()[:, hs:hs + H5], pb[4][:], AF.Sigmoid, bias=cc(C0), scale=cc(C1)),
                        reads=[("pb", 4), ("cst",)], writes=[("ags", sx)])
                    if half == 1:
                        ev_first.append(t_k)
                    pr.op("tensor", lambda t, half=half: t.matmul(
                        pb_misc.ap()[0:1, :], s_ones_c[:], hy_sb[32 + half][:],
                        start=True, stop=True),
                        reads=[("hy", 32 + half), ("onesc",), ("pbm",)], writes=[("pbm",)], weak=True)
                    t_rs = pr.op("vector", lambda v, hs=hs: v.tensor_copy(
                        A["row"].ap()[0:1, hs:hs + H5], pb_misc.ap()[0:1, :]),
                        reads=[("pbm",)], writes=[("rowa", sx)])
                    pr.cover([("hy", 32 + half), ("onesc",)], [t_rs])
                pr.cover([("winb", (ht * 4 + q) % NWIN) for q in range(4)] +
                         [("xnT", d) for d in range(NDT)], ev_first[:1])
                pr.op("scalar", lambda s: s.activation(
                    A["row"][:], A["row"][:], AF.Ln, bias=cc(EPSQ, 1), scale=cc(I128, 1)),
                    reads=[("rowa", sx), ("cst",)], writes=[("rowa", sx)])
                pr.op("scalar", lambda s: s.activation(
                    A["row"][:], A["row"][:], AF.Exp, bias=cc(C0, 1), scale=cc(CM05, 1)),
                    reads=[("rowa", sx), ("cst",)], writes=[("rowb", sx)])
                pr.op("vector", lambda v: v.tensor_copy(
                    A["a0"][:], A["as_"][:]),
                    reads=[("aas", sx)], writes=[("aa0", sx)])
                pr.op("vector", lambda v: v.memset(A["a0"].ap()[:, 0:TCA:SCH], 0.0),
                    reads=[("aa0", sx)], writes=[("aa0", sx)])
                pr.op("vector", lambda v: v.tensor_tensor(
                    A["a1"][:], A["as_"][:], A["a0"][:], OP.subtract),
                    reads=[("aas", sx), ("aa0", sx)], writes=[("aa1", sx)])
                pr.op("vector", lambda v: v.tensor_tensor_scan(
                    A["pref"][:], A["a0"][:], A["a1"][:], 1.0, OP.mult, OP.add),
                    reads=[("aa0", sx), ("aa1", sx)], writes=[("apref", sx)])
                pr.op("vector", lambda v: v.tensor_scalar(
                    A["pref"][:], A["pref"][:], 1e6, 1.0, OP.mult, OP.min),
                    reads=[("apref", sx)], writes=[("apref", sx)])
                pr.op("vector", lambda v: v.tensor_tensor(
                    A["a1"][:], A["kc"][:], A["vf"][:], OP.mult),
                    reads=[("akc", sx), ("avf", sx)], writes=[("aa1", sx)])
                pr.op("vector", lambda v: v.tensor_tensor(
                    A["pref"][:], A["a1"][:], A["pref"][:], OP.mult),
                    reads=[("aa1", sx), ("apref", sx)], writes=[("apref", sx)])
                if ch == 0:
                    pr.op("vector", lambda v: v.tensor_tensor_scan(
                        A["a0"][:], A["as_"][:], A["pref"][:], 0.0, OP.mult, OP.add),
                        reads=[("aas", sx), ("apref", sx)], writes=[("aa0", sx)])
                else:
                    pr.op("vector", lambda v, ht=ht: v.tensor_tensor_scan(
                        A["a0"][:], A["as_"][:], A["pref"][:], s_state.ap()[:, ht:ht + 1],
                        OP.mult, OP.add),
                        reads=[("aas", sx), ("apref", sx), ("state", ht)], writes=[("aa0", sx)])
                if ch < S // TCA - 1:
                    pr.op("vector", lambda v, ht=ht: v.tensor_copy(
                        s_state.ap()[:, ht:ht + 1], A["a0"].ap()[:, TCA - 1:TCA]),
                        reads=[("aa0", sx)], writes=[("state", ht)])
                pr.op("vector", lambda v: v.tensor_tensor(
                    A["kc"][:], A["qf"][:], A["a0"][:], OP.mult),
                    reads=[("aqf", sx), ("aa0", sx)], writes=[("akc", sx)])
                pr.op("vector", lambda v: v.tensor_tensor(
                    A["kc"][:], A["kc"][:], A["gs"][:], OP.mult),
                    reads=[("akc", sx), ("ags", sx)], writes=[("akc", sx)])
                pr.op("vector", lambda v: v.tensor_tensor(
                    A["a1"][:], A["gs"][:], A["vf"][:], OP.mult),
                    reads=[("ags", sx), ("avf", sx)], writes=[("aa1", sx)])
                pr.op("vector", lambda v: v.tensor_tensor(
                    A["a1"][:], A["vf"][:], A["a1"][:], OP.subtract),
                    reads=[("avf", sx), ("aa1", sx)], writes=[("aa1", sx)])
                emit_dq(18)
            emit_bcast_y(NH - 1)

            for thalf in range(2):
                for oc in range(4):
                    for cg in range(4):
                        pr.dma("sync", lambda h, oc=oc, cg=cg: h.dma_start(
                            out=wsm_buf[cg].ap().rearrange("p (a c) -> p a c", a=4),
                            in_=wout_bfh[cg * 4:(cg + 1) * 4, :, oc * TC:(oc + 1) * TC].transpose([1, 0, 2])),
                            reads=[("wot", cg // 2)], writes=[("wsmr", 2 * cg), ("wsmr", 2 * cg + 1)],
                            sem=f"d_wsm{cg}")
                    def wo_emit(t, thalf=thalf, oc=oc):
                        last = None
                        for ct in range(NDT):
                            for tt in range(NTT):
                                tg = thalf * H5 + tt * P
                                last = t.matmul(
                                    pb[tt][:],
                                    hy_sb[2 * ct + tg // H5].ap()[:, (tg % H5):(tg % H5) + P],
                                    wsm_buf[ct // 4].ap()[:, (ct % 4) * TC:(ct % 4 + 1) * TC],
                                    start=(ct == 0), stop=(ct == NDT - 1))
                        return last
                    pr.op("tensor", wo_emit,
                          reads=[("hy", c2) for c2 in range(2 * NDT)] +
                                [("wsmr", r) for r in range(8)] +
                                [("pb", tt) for tt in range(NTT)],
                          writes=[("pb", tt) for tt in range(NTT)], weak=True)
                    ev = []
                    for tt in range(NTT):
                        sl = tt % 2
                        t_cp = pr.op("vector", lambda v, tt=tt, sl=sl: v.tensor_copy(
                            A["pref"].ap()[:, sl * H5: sl * H5 + H5], pb[tt][:]),
                            reads=[("pb", tt)], writes=[("apref", sl, "wo")])
                        ev.append(t_cp)
                        pr.dma("sync", lambda h, thalf=thalf, oc=oc, tt=tt, sl=sl, base=base: h.dma_start(
                            out=x1_hbm[base + thalf * H5 + tt * P: base + thalf * H5 + (tt + 1) * P,
                                       oc * TC:(oc + 1) * TC],
                            in_=A["pref"].ap()[:, sl * H5: sl * H5 + H5]),
                            reads=[("apref", sl, "wo")], writes=[("x1h", ch, thalf, oc, tt)],
                            sem="d_x1h")
                    pr.cover([("hy", c2) for c2 in range(2 * NDT)] +
                             [("wsmr", r) for r in range(8)], ev)
                    emit_dq(9)

        wgu_r = wgu_bf.rearrange("(dt p) (r io) -> dt p r io", p=P, io=IO)
        wdn_r = wdn_bf.rearrange("(it p) n -> it p n", p=P)
        for ch in range(NCH):
            base = ch * TC
            for tt in range(NTT):
                pr.dma("sync", lambda h, tt=tt, base=base: h.dma_start(
                    out=x_sb[tt][:], in_=x1_hbm[base + tt * P: base + (tt + 1) * P, :]),
                    writes=[("x", tt)],
                    extra=[pr.all_of("d_x1h"), pr.all_of("d_out")], sem=f"d_x{tt}")
                pr.dma("sync", lambda h, tt=tt, base=base: h.dma_start(
                    out=ATT[0]["a0"][:], in_=x_in[base + tt * P: base + (tt + 1) * P, 0:TCA]),
                    reads=[("aa0", 0)], writes=[("aa0", 0)], sem="d_xa0")
                pr.dma("sync", lambda h, tt=tt, base=base: h.dma_start(
                    out=ATT[0]["pref"][:], in_=x_in[base + tt * P: base + (tt + 1) * P, TCA:D]),
                    reads=[("apref", 0)], writes=[("apref", 0)], sem="d_xa1")
                pr.op("vector", lambda v, tt=tt: v.tensor_tensor(
                    x_sb[tt].ap()[:, 0:TCA], x_sb[tt].ap()[:, 0:TCA], ATT[0]["a0"][:], OP.add),
                    reads=[("x", tt), ("aa0", 0)], writes=[("x", tt)])
                pr.op("vector", lambda v, tt=tt: v.tensor_tensor(
                    x_sb[tt].ap()[:, TCA:D], x_sb[tt].ap()[:, TCA:D], ATT[0]["pref"][:], OP.add),
                    reads=[("x", tt), ("apref", 0)], writes=[("x", tt)])
            ln_pass()
            transpose_pass()
            emit_dq(20)

            for pair in range(NPAIR):
                emit_dq_upto(gup_need[min(pair // 2 + 1, NCCP - 1)])
                pbg = pb[0] if pair % 2 == 0 else pb[2]
                pbu = pb[1] if pair % 2 == 0 else pb[3]
                kg = ("pb", 0 if pair % 2 == 0 else 2)
                ku = ("pb", 1 if pair % 2 == 0 else 3)
                for q in range(4):
                    vs = (pair * 4 + q) % 8
                    tb, vo = vs % 4, (vs // 4) * 1024
                    extraw = [("winb", vs % 4)] if pair * 4 + q in (4, 5, 6, 7) else []
                    pr.dma("sync", lambda h, pair=pair, q=q, tb=tb, vo=vo: h.dma_start(
                        out=win_buf[tb].ap()[:, vo:vo + 512].rearrange("p (a c) -> p a c", a=4),
                        in_=wgu_r[q * 4:(q + 1) * 4, :, 0, pair * P:(pair + 1) * P].transpose(
                            [1, 0, 2])),
                        reads=[("wdeq", "d_wdeqg")], writes=[("winb", 8 + vs)] + extraw,
                        sem=f"d_win{tb}")
                    pr.dma("sync", lambda h, pair=pair, q=q, tb=tb, vo=vo: h.dma_start(
                        out=win_buf[tb].ap()[:, vo + 512:vo + 1024].rearrange("p (a c) -> p a c", a=4),
                        in_=wgu_r[q * 4:(q + 1) * 4, :, 1, pair * P:(pair + 1) * P].transpose(
                            [1, 0, 2])),
                        reads=[("wdeq", "d_wdeqg")], writes=[("winb", 8 + vs)],
                        sem=f"d_win{tb}")
                for q in range(4):
                    vs = (pair * 4 + q) % 8
                    tb, vo = vs % 4, (vs // 4) * 1024
                    def gu_mm(t, q=q, tb=tb, vo=vo, pbg=pbg, pbu=pbu):
                        last = None
                        for dto in range(4):
                            dt = q * 4 + dto
                            last = t.matmul(pbg[:],
                                            win_buf[tb].ap()[:, vo + dto * P:vo + (dto + 1) * P],
                                            xnT[dt].ap()[:, 0:TC], start=(dt == 0), stop=(dt == NDT - 1))
                            last = t.matmul(pbu[:],
                                            win_buf[tb].ap()[:, vo + 512 + dto * P: vo + 512 + (dto + 1) * P],
                                            xnT[dt].ap()[:, 0:TC], start=(dt == 0), stop=(dt == NDT - 1))
                        return last
                    pr.op("tensor", gu_mm,
                          reads=[("winb", 8 + vs)] + [("xnT", q * 4 + i) for i in range(4)] +
                                [kg, ku],
                          writes=[kg, ku], weak=True)
                t_si = pr.op("scalar", lambda s, pair=pair, pbg=pbg: s.activation(
                    hy_sb[pair][:], pbg[:], AF.Silu, bias=cc(C0), scale=cc(C1)),
                    reads=[kg, ("cst",)], writes=[("hy", pair)])
                t_h = pr.op("vector", lambda v, pair=pair, pbu=pbu: v.tensor_tensor(
                    hy_sb[pair][:], hy_sb[pair][:], pbu[:], OP.mult),
                    reads=[("hy", pair), ku], writes=[("hy", pair)])
                pr.cover([("winb", 8 + (pair * 4 + q) % 8) for q in range(4)] +
                         [("xnT", d) for d in range(NDT)], [t_si, t_h])
                if pair >= NPAIR - 12:
                    emit_dq(max(1, cdiv(len(dq_units) - n_gup_units, 10)))

            for hc in range(4):
                emit_dq_upto(dn_need[hc])
                since = set()
                for ip in range(NIT // 2):
                    reg = ip % 8
                    tb, half = reg % 4, reg // 4
                    pr.dma("sync", lambda h, hc=hc, ip=ip, tb=tb, half=half: h.dma_start(
                        out=wsm_buf[tb].ap()[:, half * 2 * TC: (half + 1) * 2 * TC].rearrange(
                            "p (a c) -> p a c", a=2),
                        in_=wdn_r[2 * ip:2 * ip + 2, :, hc * TC:(hc + 1) * TC].transpose([1, 0, 2])),
                        reads=[("wdeq", "d_wdeqd")], writes=[("wsmr", reg)], sem=f"d_wsm{tb}")
                    def dn_mm(t, ip=ip, tb=tb, half=half):
                        last = None
                        for k in range(2):
                            it = 2 * ip + k
                            for tt in range(NTT):
                                last = t.matmul(
                                    pb[tt][:],
                                    hy_sb[it].ap()[:, tt * P:(tt + 1) * P],
                                    wsm_buf[tb].ap()[:, (half * 2 + k) * TC:(half * 2 + k + 1) * TC],
                                    start=(it == 0), stop=(it == NIT - 1))
                        return last
                    pr.op("tensor", dn_mm,
                          reads=[("hy", 2 * ip), ("hy", 2 * ip + 1), ("wsmr", reg)] +
                                [("pb", tt) for tt in range(NTT)],
                          writes=[("pb", tt) for tt in range(NTT)], weak=True)
                    since.add(reg)
                    if (ip + 1) % 8 == 0 and ip != NIT // 2 - 1:
                        t_dr = pr.op("tensor", lambda t: t.drain(fusable=False))
                        pr.cover([("wsmr", r) for r in since], [t_dr])
                        since = set()
                ev = []
                for tt in range(NTT):
                    t_cp = pr.op("vector", lambda v, hc=hc, tt=tt: v.tensor_copy(
                        xn_sb[tt].ap()[:, hc * TC:(hc + 1) * TC], pb[tt][:]),
                        reads=[("pb", tt)], writes=[("xn", tt)])
                    t_sq = pr.op("scalar", lambda s, hc=hc, tt=tt: s.activation(
                        ATT[0]["a0"].ap()[:, 0:TC], pb[tt][:], AF.Square, bias=cc(C0), scale=cc(C1),
                        accum_out=f_ssum.ap()[:, tt * 4 + hc: tt * 4 + hc + 1]),
                        reads=[("pb", tt), ("cst",)],
                        writes=[("aa0", 0), ("fss", tt, hc)])
                    ev += [t_cp, t_sq]
                pr.cover([("hy", it) for it in range(NIT)] +
                         [("wsmr", r) for r in range(8)], ev)
            for tt in range(NTT):
                pr.op("vector", lambda v, tt=tt: v.reduce_sum(
                    f_tot[tt][:], f_ssum.ap()[:, tt * 4:(tt + 1) * 4], axis=AX.X),
                    reads=[("fss", tt, h2) for h2 in range(4)], writes=[("ftot", tt)])
                pr.op("scalar", lambda s, tt=tt: s.activation(
                    f_tot[tt][:], f_tot[tt][:], AF.Ln, bias=cc(EPSF), scale=cc(I2048)),
                    reads=[("ftot", tt), ("cst",)], writes=[("ftot", tt)])
                pr.op("scalar", lambda s, tt=tt: s.activation(
                    f_tot[tt][:], f_tot[tt][:], AF.Exp, bias=cc(C0), scale=cc(CM05)),
                    reads=[("ftot", tt), ("cst",)], writes=[("ftot", tt)])
                pr.op("vector", lambda v, tt=tt: v.scalar_tensor_tensor(
                    x_sb[tt][:], xn_sb[tt][:], f_tot[tt][:], x_sb[tt][:], OP.mult, OP.add),
                    reads=[("xn", tt), ("ftot", tt), ("x", tt)], writes=[("x", tt)])
                pr.dma("sync", lambda h, tt=tt, base=base: h.dma_start(
                    out=out_hbm[base + tt * P: base + (tt + 1) * P, :], in_=x_sb[tt][:]),
                    reads=[("x", tt)], writes=[("outh", ch, tt)], sem="d_out")

        pr.op("sync", lambda h: h.nop(), extra=[pr.all_of("d_out")])
        pr.emit(block, sems)
    return nc


def _host_layouts(w_in, w_out, gup_packed, gup_scales, dn_packed, dn_scales):
    wT = np.ascontiguousarray(w_in.T)
    win_blk = np.ascontiguousarray(
        wT.reshape(NDT, P, 5, NH, P).transpose(3, 0, 1, 2, 4).reshape(NH, NDT, P, 5 * P))
    wout_blk = np.ascontiguousarray(w_out.T.reshape(NDT, P, D))
    gup_pk = np.ascontiguousarray(gup_packed.T)
    gup_sc = np.ascontiguousarray(np.repeat(gup_scales, 16, axis=0))
    dn_pk = np.ascontiguousarray(dn_packed.T)
    dn_sc = np.ascontiguousarray(np.repeat(dn_scales, 16, axis=0))
    idm = np.eye(P, dtype=np.float32).astype(ml_dtypes.bfloat16)
    cst = np.broadcast_to(np.array(CONST_VALS + [0.0, 0.0], np.float32), (P, 12)).copy()
    return win_blk, wout_blk, gup_pk, gup_sc, dn_pk, dn_sc, idm, cst


def kernel(L0_attn_x, L0_attn_w_in, L0_attn_w_out, L0_ffn_gate_up_packed,
           L0_ffn_gate_up_scales, L0_ffn_down_packed, L0_ffn_down_scales):
    x = np.ascontiguousarray(np.asarray(L0_attn_x, dtype=np.float32))
    B, S, Dm = x.shape
    assert Dm == D
    win_blk, wout_blk, gup_pk, gup_sc, dn_pk, dn_sc, idm, cst = _host_layouts(
        np.asarray(L0_attn_w_in, np.float32), np.asarray(L0_attn_w_out, np.float32),
        np.asarray(L0_ffn_gate_up_packed, np.int32),
        np.asarray(L0_ffn_gate_up_scales, np.float16),
        np.asarray(L0_ffn_down_packed, np.int32),
        np.asarray(L0_ffn_down_scales, np.float16))
    nc = build_nc(S)
    shared = dict(win_blk=win_blk, wout_blk=wout_blk, gup_pk=gup_pk, gup_sc=gup_sc,
                  dn_pk=dn_pk, dn_sc=dn_sc, idm=idm, cst=cst)
    in_maps = [dict(x_in=np.ascontiguousarray(x[c]), **shared) for c in range(B)]
    res = run_bass_kernel_spmd(nc, in_maps, core_ids=list(range(B)))
    out = np.stack([np.asarray(res.results[c]["out"], dtype=np.float32) for c in range(B)])
    return out
